# version 01
# baseline (speedup 1.0000x reference)
import os
import numpy as np
import ml_dtypes
import concourse.bass as bass
import concourse.mybir as mybir
from concourse.bass_utils import run_bass_kernel_spmd

F32 = mybir.dt.float32
BF16 = mybir.dt.bfloat16
I32 = mybir.dt.int32
AF = mybir.ActivationFunctionType
ALU = mybir.AluOpType
bf16 = ml_dtypes.bfloat16

D = 2048
LSEQ = 2048
NT = 512
NH, HQK, HV = 32, 32, 64
DK, DV, FF = 1024, 2048, 5632
CH = 32
NCH = NT // CH
NBLK = NT // 128
EPS = 1e-5

PRE, A1, A2, A3, A4, A5, F1, F2 = range(8)

N_TILES = int(os.environ.get("MK_NTILES", "0")) or 8
DEBUG = os.environ.get("MK_DEBUG", "")
CUT = os.environ.get("MK_CUT", "")
TPV = os.environ.get("MK_TPV", "")
TPN = int(os.environ.get("MK_TPN", "32"))
GPN = int(os.environ.get("MK_GPN", "16"))
GDST = os.environ.get("MK_GDST", "")
SBLK = int(os.environ.get("MK_SBLK", "99"))
SPART = os.environ.get("MK_SPART", "")

BON = 0
MUX = 8
GNW = 24
GNB = 25
SGA = 26
SUP = 70
SDN = 114
EPSC = 130
ZRO = 131
NCOLS = 132
MSU = 0
MSD = 32
IDE = 64
EPA = 192
ONE = 704
NBF = 706


class Em:
    ENGINES = ("sync", "gpsimd", "vector", "scalar", "tensor")

    def __init__(self, nc):
        self.nc = nc
        self.items = {e: [] for e in self.ENGINES}
        self.sems = {}
        self.cnt = {}
        self.last_wait = {e: {} for e in self.ENGINES}

    def _sem(self, name):
        if name not in self.sems:
            self.sems[name] = self.nc.alloc_semaphore(name)
            self.cnt[name] = 0
        return self.sems[name]

    def mark(self, ch):
        self._sem(ch)
        return self.cnt[ch]

    def wait(self, eng, ch, val=None):
        self._sem(ch)
        v = self.cnt[ch] if val is None else val
        if v <= 0 or self.last_wait[eng].get(ch, -1) >= v:
            return
        self.last_wait[eng][ch] = v
        self.items[eng].append(("w", ch, v))

    ECH = {"vector": "dve", "scalar": "act", "gpsimd": "pool"}

    def op(self, eng, fn, inc=None, inc_n=1, dma=False):
        ch = self.ECH.get(eng)
        if ch is not None and not dma:
            assert inc is None or inc == ch, (eng, inc)
            self.wait(eng, ch)
            inc, inc_n = ch, 1
        if inc:
            self._sem(inc)
            self.cnt[inc] += inc_n
        self.items[eng].append(("o", fn, inc, inc_n))

    def dma(self, eng, out, in_, inc):
        self.op(eng, lambda e, o=out, i=in_: e.dma_start(out=o, in_=i), inc, 16,
                dma=True)

    def fence(self, consumers, producers):
        for eng in consumers:
            for ch in producers:
                self.wait(eng, ch)

    def emit(self):
        with self.nc.Block() as block:
            for eng in self.ENGINES:
                items = self.items[eng]
                if not items:
                    continue

                def mk(items=items):
                    def body(e):
                        for it in items:
                            if it[0] == "w":
                                e.wait_ge(self.sems[it[1]], it[2])
                            else:
                                ins = it[1](e)
                                if it[2] is not None:
                                    ins.then_inc(self.sems[it[2]], it[3])
                    return body
                getattr(block, eng)(mk())


class Plan:
    def __init__(self, nc, base, size):
        self.nc = nc
        self.base = base
        self.size = size
        self.items = []

    def tensor(self, name, shape, dtype, s, e, align=128):
        isz = 2 if dtype == BF16 else 4
        nb = int(np.prod(shape[1:])) * isz if len(shape) > 1 else isz
        nb = (nb + align - 1) // align * align
        spans = sorted((it[1], it[1] + it[2]) for it in self.items
                       if not (e < it[3] or s > it[4]))
        off = 0
        for a, b in spans:
            if off + nb <= a:
                break
            off = max(off, b)
        assert off + nb <= self.size, \
            f"sbuf overflow: {name} needs {nb} at {off}, cap {self.size}"
        self.items.append((name, off, nb, s, e))
        return self.nc.alloc_sbuf_tensor_at(name, list(shape), dtype,
                                            offset=self.base + off)


def build(n_tiles=8):
    nc = bass.Bass("TRN2", target_bir_lowering=False)

    def din(name, shape, dt=F32):
        return nc.dram_tensor(name, list(shape), dt, kind="ExternalInput").ap()

    x_d = din("x", [2, D, LSEQ])
    wx1_d = din("wx1", [128, 16 * 160])
    wx2a_d = din("wx2a", [128, D])
    wx2b_d = din("wx2b", [32, D])
    ww1_d = din("ww1", [128, 16 * 64])
    ww2_d = din("ww2aug", [65, DK])
    wr_d = din("wr", [D, DK])
    wk_d = din("wk", [D, DK])
    wv_d = din("wv", [D, DV])
    wg_d = din("wg", [D, DV])
    wo_d = din("wo", [DV, D])
    wqg_d = din("wqg", [D, FF], I32)
    wqu_d = din("wqu", [D, FF], I32)
    wqd_d = din("wqd", [FF, D], I32)
    cols_d = din("cols", [128, NCOLS])
    cbf_d = nc.dram_tensor("cbf", [128, NBF], BF16, kind="ExternalInput").ap()
    gpat_d = nc.dram_tensor("gpat", [32, 2048], BF16, kind="ExternalInput").ap()

    y_d = nc.dram_tensor("y", [2, D, LSEQ], F32, kind="ExternalOutput").ap()

    def scr(name, shape):
        return nc.dram_tensor(name, list(shape), BF16, kind="Internal").ap()

    wr_sc = scr("wr_sc", [8, 16, 128, 128])
    wk_sc = scr("wk_sc", [8, 16, 128, 128])
    wv_sc = scr("wv_sc", [4, 16, 128, 512])
    wg_sc = scr("wg_sc", [16, 16, 128, 128])
    wo_sc = scr("wo_sc", [16, 16, 128, 128])
    gt_sc = scr("gt_sc", [44, 16, 128, 128])
    up_sc = scr("up_sc", [44, 16, 128, 128])
    dn_sc = scr("dn_sc", [16, 44, 128, 128])

    cap = nc.sbuf_bytes_remaining
    lo, hi = nc.bump_sbuf(cap - 256)
    pl = Plan(nc, lo, hi - lo)
    P0, P1 = PRE, F2

    wx1_s = pl.tensor("wx1_s", [128, 16 * 160], BF16, P0, P1)
    wx2a_s = pl.tensor("wx2a_s", [128, D], BF16, P0, P1)
    wx2b_s = pl.tensor("wx2b_s", [32, D], BF16, P0, P1)
    ww1_s = pl.tensor("ww1_s", [128, 16 * 64], BF16, P0, P1)
    ww2_s = pl.tensor("ww2_s", [65, DK], BF16, P0, P1)
    cols_s = pl.tensor("cols_s", [128, NCOLS], F32, P0, P1)
    cbf_s = pl.tensor("cbf_s", [128, NBF], BF16, P0, P1)
    gpat_s = pl.tensor("gpat_s", [32, 2048], BF16, P0, P1)
    tstate = [pl.tensor(f"tstate{i}", [128, 512], BF16, P0, P1)
              for i in range(2)]
    psi_f = pl.tensor("psi_f", [128, 8 * 18], F32, A3, A4)
    psi_b = pl.tensor("psi_b", [128, 8 * 18], BF16, A3, A4)
    zb0 = pl.tensor("zb0", [128, NT], BF16, A2, A2)
    zb1 = pl.tensor("zb1", [32, NT], BF16, A2, A2)
    tanha = pl.tensor("tanha", [65, NT], BF16, A2, A2)
    rows_b = pl.tensor("rows_b", [1, 1024], BF16, P0, P1)
    rows_f = pl.tensor("rows_f", [1, 1024], F32, P0, P1)
    gnm_s = pl.tensor("gnm_s", [32, 512], BF16, A4, A4)
    gnr_s = pl.tensor("gnr_s", [32, 512], BF16, A4, A4)
    gnscr = pl.tensor("gnscr", [32, 512], F32, A4, A4)
    dm1 = pl.tensor("dm1", [128, 512], F32, P0, P1)
    dm2 = pl.tensor("dm2", [128, 512], F32, P0, P1)
    dmb1 = pl.tensor("dmb1", [128, 512], BF16, P0, P1)
    dmb2 = pl.tensor("dmb2", [128, 512], BF16, P0, P1)
    dmb3 = pl.tensor("dmb3", [128, 512], BF16, P0, P1)
    dmb4 = pl.tensor("dmb4", [128, 512], BF16, P0, P1)
    dmb5 = pl.tensor("dmb5", [128, 512], BF16, P0, P1)
    carry_s = pl.tensor("carry_s", [128, 16], BF16, P0, P1)

    xin = pl.tensor("xin", [128, 16 * 512], F32, A1, A1)
    xb16 = pl.tensor("xb16", [128, 16 * 512], BF16, A1, A1)
    xln = pl.tensor("xln", [128, 16 * 513], BF16, A1, A2)
    delta = pl.tensor("delta", [128, 16 * 512], BF16, A2, A2)
    tmp_b = pl.tensor("tmp_b", [128, 16 * 512], BF16, A2, A2)
    xmix = [pl.tensor(f"xmix{i}", [128, 16 * 512], BF16, A2, A3 if i == 0 else A2)
            for i in range(2)]
    vtok = pl.tensor("vtok", [128, 4 * 2048], BF16, A2, A4)
    sg_s = pl.tensor("sg_s", [128, 16 * 512], BF16, A3, A4)
    r_raw = pl.tensor("r_raw", [128, 8 * 512], BF16, A2, A4)
    k_raw = pl.tensor("k_raw", [128, 8 * 512], BF16, A2, A3)
    wexp = pl.tensor("wexp", [128, 8 * 512], F32, A2, A3)
    p512 = pl.tensor("p512", [128, 8 * 512], F32, A3, A3)
    fk_s = pl.tensor("fk_s", [128, 8 * 512], BF16, A3, A3)
    fr_s = pl.tensor("fr_s", [128, 8 * 512], BF16, A3, A3)
    khfm = pl.tensor("khfm", [128, 8 * 512], BF16, A3, A3)
    rt_s = pl.tensor("rt_s", [128, 8 * 512], BF16, A3, A4)
    kt_s = pl.tensor("kt_s", [128, 8 * 512], BF16, A3, A4)
    kht = pl.tensor("kht", [128, 4 * 1024], BF16, A3, A4)
    kb_s = pl.tensor("kb_s", [128, 8 * 512], BF16, A3, A4)
    st_s = pl.tensor("st_s", [128, 1024], BF16, A4, A4)
    ob16 = pl.tensor("ob16", [128, 16 * 512], BF16, A4, A4)
    oc_b = pl.tensor("oc_b", [128, 16 * 512], BF16, A4, A4)
    oprime = pl.tensor("oprime", [128, 16 * 512], BF16, A4, A5)
    ao_b = pl.tensor("ao_b", [128, 16 * 512], BF16, A5, A5)
    xrel = pl.tensor("xrel", [128, 16 * 512], F32, A5, A5)
    xprime = pl.tensor("xprime", [128, 16 * 512], F32, A5, F2)
    xln3 = pl.tensor("xln3", [128, 16 * 512], BF16, F1, F1)
    gu_s = pl.tensor("gu_s", [128, 44 * 512], BF16, F1, F2)
    dnb = pl.tensor("dnb", [128, 16 * 512], BF16, F2, F2)
    y_s = pl.tensor("y_s", [128, 16 * 512], F32, F2, F2)
    wcol = [pl.tensor(f"wcol{i}", [128, 2048], BF16, A2, A5) for i in range(2)]
    gcol = [pl.tensor(f"gcol{i}", [128, 2048], BF16, F1, F1) for i in range(2)]
    ucol = [pl.tensor(f"ucol{i}", [128, 2048], BF16, F1, F1) for i in range(2)]
    dcol = [pl.tensor(f"dcol{i}", [128, 44 * 128], BF16, F2, F2)
            for i in range(2)]
    pst_i = [pl.tensor(f"pst_i{i}", [128, FF], I32, PRE, PRE) for i in range(2)]
    pst_b = [pl.tensor(f"pst_b{i}", [128, FF], BF16, PRE, PRE)
             for i in range(2)]
    pst_f = [pl.tensor(f"pst_f{i}", [128, 2816], F32, PRE, PRE)
             for i in range(2)]
    pst_o = [pl.tensor(f"pst_o{i}", [128, 2816], BF16, PRE, PRE)
             for i in range(2)]

    tp_ps = nc.alloc_psum_tensor("tp_ps", [128, 1024], BF16)
    pb = [nc.alloc_psum_tensor(f"pb{i}", [128, 512], F32) for i in range(7)]

    em = Em(nc)

    def col(idx):
        return cols_s[:, idx:idx + 1]

    def TT(eng, out, a, b, op, inc=None):
        em.op(eng, lambda e, o=out, x=a, y=b, p=op: e.tensor_tensor(o, x, y, p),
              inc)

    def STT(eng, out, in0, scalar, in1, op0, op1, inc=None):
        em.op(eng, lambda e, o=out, i0=in0, s=scalar, i1=in1, p0=op0, p1=op1:
              e.scalar_tensor_tensor(o, i0, s, i1, p0, p1), inc)

    def TS(eng, out, in0, s1, op0, inc=None, s2=None, op1=None):
        if op1 is None:
            em.op(eng, lambda e, o=out, i=in0, s=s1, p=op0:
                  e.tensor_scalar(o, i, s, None, p), inc)
        else:
            em.op(eng, lambda e, o=out, i=in0, a=s1, b=s2, p0=op0, p1=op1:
                  e.tensor_scalar(o, i, a, b, p0, p1), inc)

    def ACT(out, in_, func, bias=0.0, scale=1.0, inc=None):
        em.op("scalar", lambda e, o=out, i=in_, f=func, b=bias, s=scale:
              e.activation(o, i, f, bias=b, scale=s), inc)

    def CP(eng, out, in_, inc=None):
        em.op(eng, lambda e, o=out, i=in_: e.tensor_copy(o, i), inc)

    def MM(out, lhsT, rhs, start, stop, tp=None, inc=None):
        em.op("tensor", lambda e, o=out, l=lhsT, r=rhs, s0=start, s1=stop, t=tp:
              e.matmul(o, l, r, start=s0, stop=s1, tile_position=t,
                       skip_group_check=True), inc)

    def TRANS(out, in_, inc=None):
        em.op("tensor", lambda e, o=out, i=in_:
              e.transpose(o, i, cbf_s[:, IDE:IDE + 128]), inc)

    def MEMSET(eng, ap, v, inc=None):
        em.op(eng, lambda e, a=ap, vv=v: e.memset(a, vv), inc)

    def pc(t, i, c0=0, c1=512, w=512):
        return t[:, i * w + c0: i * w + c1]

    small = [(wx1_d, wx1_s, 16 * 160, 128), (wx2a_d, wx2a_s, D, 128),
             (wx2b_d, wx2b_s, D, 32), (ww1_d, ww1_s, 16 * 64, 128),
             (ww2_d, ww2_s, DK, 65)]
    for i, (dsrc, sdst, ncols, nprt) in enumerate(small):
        stg = pst_f[i % 2]
        ch = f"pld{i % 2}"
        em.wait("sync", "dve", em.mark("dve"))
        em.dma("sync", stg[0:nprt, 0:ncols], dsrc, ch)
        em.wait("vector", ch)
        CP("vector", sdst[0:nprt, 0:ncols], stg[0:nprt, 0:ncols], inc="dve")
    em.dma("sync", cols_s[:], cols_d, "pldc")
    em.dma("sync", cbf_s[:], cbf_d, "pldc")
    em.dma("sync", gpat_s[:], gpat_d, "pldc")
    MEMSET("vector", tstate[0][:], 0.0, inc="dve")
    MEMSET("vector", tstate[1][:], 0.0, inc="dve")

    pre_ready = {}
    st_gate = {0: None, 1: None}

    def conv_attn(name, dsrc, dscr, nm, w=128):
        for ki in range(16):
            s = ki % 2
            stch = f"pst_{name}{s}"
            em.wait("sync", "dve", em.mark("dve") - (1 if ki >= 1 else 0))
            if ki == 0:
                em.wait("sync", "pool", em.mark("pool"))
            ncol = dsrc.shape[1]
            em.dma("sync", pst_f[s][:, 0:ncol],
                   dsrc[128 * ki:128 * ki + 128, :], f"pld{s}")
            em.wait("vector", f"pld{s}")
            if st_gate[s]:
                em.wait("vector", st_gate[s][0], st_gate[s][1])
            CP("vector", pst_o[s][:, 0:ncol], pst_f[s][:, 0:ncol], inc="dve")
            em.wait("scalar", "dve")
            nmm = ncol // w
            src = pst_o[s][:, 0:ncol].rearrange("p (m c) -> p m c", m=nmm)
            dst = dscr[:, ki, :, :].transpose([1, 0, 2])
            em.dma("scalar", dst, src, stch)
            st_gate[s] = (stch, em.mark(stch))
        pre_ready[name] = (f"pst_{name}0", em.mark(f"pst_{name}0"),
                           f"pst_{name}1", em.mark(f"pst_{name}1"))

    conv_attn("wr", wr_d, wr_sc, 8)
    conv_attn("wk", wk_d, wk_sc, 8)
    conv_attn("wv", wv_d, wv_sc, 4, w=512)
    conv_attn("wg", wg_d, wg_sc, 16)
    conv_attn("wo", wo_d, wo_sc, 16)

    def conv_ffn(name, dsrc, dscr, nk, nm):
        for ki in range(nk):
            s = ki % 2
            stch = f"pst_{name}{s}"
            em.wait("sync", "pool", em.mark("pool") - (1 if ki >= 1 else 0))
            if ki == 0:
                em.wait("sync", "dve", em.mark("dve"))
            em.dma("sync", pst_i[s][:, 0:nm * 128],
                   dsrc[128 * ki:128 * ki + 128, :], f"pld{s}")
            em.wait("gpsimd", f"pld{s}")
            if st_gate[s]:
                em.wait("gpsimd", st_gate[s][0], st_gate[s][1])
            TS("gpsimd", pst_b[s][:, 0:nm * 128], pst_i[s][:, 0:nm * 128],
               -128.0, ALU.add, inc="pool")
            em.wait("scalar", "pool")
            src = pst_b[s][:, 0:nm * 128].rearrange("p (m c) -> p m c", m=nm)
            dst = dscr[:, ki, :, :].transpose([1, 0, 2])
            em.dma("scalar", dst, src, stch)
            st_gate[s] = (stch, em.mark(stch))
        pre_ready[name] = (f"pst_{name}0", em.mark(f"pst_{name}0"),
                           f"pst_{name}1", em.mark(f"pst_{name}1"))

    conv_ffn("gt", wqg_d, gt_sc, 16, 44)
    conv_ffn("up", wqu_d, up_sc, 16, 44)
    conv_ffn("dn", wqd_d, dn_sc, 44, 16)

    def wait_ready(eng, name):
        c0, v0, c1, v1 = pre_ready[name]
        em.wait(eng, c0, v0)
        em.wait(eng, c1, v1)

    def ln_stats_apply(cast_t, out_write):
        onecol = cbf_s[:, ONE:ONE + 1]
        onerow = cbf_s[0:1, ONE:ONE + 1].broadcast_to([1, 128])
        em.fence(("tensor",), ("dve", "pool", "act"))
        for p in range(16):
            MM(pb[0][0:1, :], onecol, pc(cast_t, p), p == 0, p == 15,
               inc="pe" if p == 15 else None)
        for p in range(16):
            sq = dmb3 if p % 2 == 0 else dmb2
            em.wait("scalar", "pe", max(0, em.mark("pe") - 1))
            ACT(sq[:], pc(cast_t, p), AF.Square, inc="act")
            em.wait("tensor", "act")
            MM(pb[1][0:1, :], onecol, sq[:], p == 0, p == 15, inc="pe")
        em.wait("scalar", "pe")
        ACT(rows_b[0:1, 0:512], pb[0][0:1, :], AF.Copy, scale=1.0 / D,
            inc="act")
        ACT(rows_f[0:1, 0:512], rows_b[0:1, 0:512], AF.Square, inc="act")
        em.wait("vector", "act")
        em.op("vector", lambda e: e.scalar_tensor_tensor(
            rows_f[0:1, 512:1024], pb[1][0:1, :], 1.0 / D,
            rows_f[0:1, 0:512], ALU.mult, ALU.subtract), inc="dve")
        em.wait("scalar", "dve")
        ACT(rows_f[0:1, 0:512], rows_f[0:1, 512:1024], AF.Ln,
            bias=cols_s[0:1, EPSC:EPSC + 1], inc="act")
        ACT(rows_b[0:1, 512:1024], rows_f[0:1, 0:512], AF.Exp, scale=-0.5,
            inc="act")
        em.wait("tensor", "act")
        MM(pb[2][:], onerow, rows_b[0:1, 0:512], True, True)
        MM(pb[3][:], onerow, rows_b[0:1, 512:1024], True, True, inc="pe")
        em.fence(("vector", "gpsimd"), ("pe",))
        for p in range(16):
            TT("vector", dmb1[:], pc(cast_t, p), pb[2][:], ALU.subtract,
               inc="dve")
            out_write(p)

    dbg = {}

    def dump(tag, ap, shape, ti, dt=BF16):
        if tag not in DEBUG.split(",") or ti != 0:
            return
        name = "dbg_" + tag
        t = nc.dram_tensor(name, list(shape), dt, kind="ExternalOutput").ap()
        dbg[name] = True
        em.fence(("gpsimd",), ("dve", "act", "pool", "pe"))
        em.dma("gpsimd", t, ap, "st")
        em.fence(("vector", "scalar", "gpsimd", "tensor", "sync"), ("st",))

    sidx = [0]

    def cut_out():
        em.fence(("gpsimd",), ("dve", "act", "pool", "pe"))
        em.dma("gpsimd", y_d[0, 0:128, 0:512], dm1[:], "st")


    def tile(ti):
        seq_start = (ti % 4) == 0
        b = ti // 4
        t0 = (ti % 4) * NT

        em.fence(("vector", "gpsimd", "scalar", "tensor", "sync"),
                 tuple(em.cnt.keys()))
        xle = xln[:].rearrange("p (c k) -> p c k", k=513)
        if seq_start:
            MEMSET("vector", xle[:, :, 0:1], 0.0, inc="dve")
        else:
            CP("vector", xle[:, :, 0:1], carry_s[:].unsqueeze(2), inc="dve")
        em.wait("gpsimd", "st")
        src = x_d[b, :, t0:t0 + NT].rearrange("(n p) c -> p n c", p=128)
        em.dma("gpsimd", xin[:].rearrange("p (n c) -> p n c", n=16), src,
               "xld1")
        em.fence(("scalar", "vector", "tensor"), ("xld1",))
        for p in range(16):
            ACT(pc(xb16, p), pc(xin, p), AF.Copy, inc="act")
        em.fence(("tensor", "vector"), ("act",))

        def ln1_write(p):
            TT("vector", xln[:, p * 513 + 1:p * 513 + 513], dmb1[:], pb[3][:],
               ALU.mult, inc="dve")
        ln_stats_apply(xb16, ln1_write)
        CP("vector", carry_s[:].unsqueeze(2), xle[:, :, 512:513], inc="dve")
        dump("xln", xln[:], [128, 16 * 513], ti)

        if CUT == "a1":
            return cut_out()
        em.fence(("vector",), ("dve",))
        TT("vector", delta[:].rearrange("p (c k) -> p c k", k=512),
           xle[:, :, 0:512], xle[:, :, 1:513], ALU.subtract, inc="dve")
        for p in range(16):
            STT("vector", pc(tmp_b, p), pc(delta, p), col(MUX + p),
                xln[:, p * 513 + 1:p * 513 + 513], ALU.mult, ALU.add,
                inc="dve")
        em.fence(("tensor",), ("act", "dve"))
        for m in range(2):
            nm = 128 if m == 0 else 32
            for ki in range(16):
                MM(pb[4 + m][0:nm, :],
                   wx1_s[:, ki * 160 + m * 128:ki * 160 + m * 128 + nm],
                   pc(tmp_b, ki), ki == 0, ki == 15,
                   inc="pe" if ki == 15 else None)
        em.wait("scalar", "pe")
        for m, (zt, nm) in enumerate(((zb0, 128), (zb1, 32))):
            em.wait("scalar", "dve")
            ACT(dm1[0:nm, :], pb[4 + m][0:nm, :], AF.Exp, scale=2.0, inc="act")
            em.wait("vector", "act")
            TS("vector", dm2[0:nm, :], dm1[0:nm, :], 1.0, ALU.add, inc="dve")
            em.op("vector", lambda e, nm=nm:
                  e.reciprocal(dm1[0:nm, :], dm2[0:nm, :]), inc="dve")
            TS("vector", zt[0:nm, :], dm1[0:nm, :], -2.0, ALU.mult, inc="dve",
               s2=1.0, op1=ALU.add)
        em.fence(("tensor",), ("dve",))

        mus_tt = {}
        mus_add = {}
        mix_done = {}

        def mus(i, dst_mix):
            em.fence(("tensor",), ("act", "dve"))
            if i < 4:
                lhs, base = wx2a_s, 32 * i
                zsl = zb0[32 * i:32 * i + 32, :]
            else:
                lhs, base = wx2b_s, 0
                zsl = zb1[0:32, :]
            for m in range(16):
                bank = m % 2
                if m - 2 in mus_tt:
                    em.wait("tensor", "dve", mus_tt[m - 2])
                MM(pb[bank][:], lhs[base:base + 32, m * 128:m * 128 + 128],
                   zsl, True, True, tp=(base, 0), inc="pe")
                em.wait("vector", "pe")
                if m in mus_add:
                    em.wait("vector", "pool", mus_add[m])
                TT("vector", pc(tmp_b, m), pb[bank][:], pc(delta, m), ALU.mult)
                mus_tt[m] = em.mark("dve")
                em.wait("gpsimd", "dve")
                TT("gpsimd", pc(dst_mix, m), pc(tmp_b, m),
                   xln[:, m * 513 + 1:m * 513 + 513], ALU.add)
                mus_add[m] = em.mark("pool")
            mix_done[i] = em.mark("pool")

        def proj(dscr, name, nm, mix, dst, mix_i=None, cap=99):
            nm = min(nm, cap)
            wait_ready("sync", name)
            if mix_i is not None:
                em.wait("tensor", "pool", mix_done[mix_i])
            evm = {}
            em.fence(("tensor",), ("act",))
            for m in range(nm):
                ch = f"wld{m % 2}"
                em.wait("sync", "pewc", max(0, em.mark("pewc") - 1))
                em.dma("sync", wcol[m % 2][:].rearrange("p (k c) -> p k c", k=16),
                       dscr[m].rearrange("k p c -> p k c"), ch)
                em.wait("tensor", ch)
                bank = 4 + (m % 2)
                if m - 2 in evm:
                    em.wait("tensor", "act", evm[m - 2])
                for ki in range(16):
                    MM(pb[bank][:], wcol[m % 2][:, ki * 128:ki * 128 + 128],
                       pc(mix, ki), ki == 0, ki == 15,
                       inc="pewc" if ki == 15 else None)
                em.wait("scalar", "pewc")
                if GDST and dst is sg_s:
                    ACT(dmb4[:], pb[bank][:], AF.Copy, inc="act")
                else:
                    ACT(pc(dst, m), pb[bank][:], AF.Copy, inc="act")
                evm[m] = em.mark("act")

        mus(0, xmix[0])
        proj(wr_sc, "wr", 8, xmix[0], r_raw, mix_i=0)
        mus(1, xmix[1])
        em.wait("tensor", "pool", mix_done[1])
        for ki in range(16):
            MM(pb[6][0:64, :], ww1_s[:, ki * 64:ki * 64 + 64], pc(xmix[1], ki),
               ki == 0, ki == 15, inc="pe" if ki == 15 else None)
        em.wait("scalar", "pe")
        em.wait("scalar", "dve")
        ACT(dm1[0:64, :], pb[6][0:64, :], AF.Exp, scale=2.0, inc="act")
        em.wait("vector", "act")
        TS("vector", dm2[0:64, :], dm1[0:64, :], 1.0, ALU.add, inc="dve")
        em.op("vector", lambda e: e.reciprocal(dm1[0:64, :], dm2[0:64, :]),
              inc="dve")
        TS("vector", tanha[0:64, :], dm1[0:64, :], -2.0, ALU.mult, inc="dve",
           s2=1.0, op1=ALU.add)
        MEMSET("vector", tanha[64:65, :], 1.0, inc="dve")
        em.fence(("tensor",), ("dve",))
        wem = {}
        em.fence(("tensor",), ("act",))
        for m in range(8):
            bank = 4 + (m % 2)
            if m - 2 in wem:
                em.wait("tensor", "act", wem[m - 2])
            MM(pb[bank][:], ww2_s[:, m * 128:m * 128 + 128], tanha[:],
               True, True, inc="pe")
            em.wait("scalar", "pe")
            ACT(pc(wexp, m), pb[bank][:], AF.Exp, inc="act")
            wem[m] = em.mark("act")
        mus(2, xmix[0])
        proj(wk_sc, "wk", 8, xmix[0], k_raw, mix_i=2)
        mus(3, xmix[1])
        wait_ready("sync", "wv")
        em.wait("tensor", "pool", mix_done[3])
        em.fence(("tensor",), ("dve", "act"))
        for vs in range(4):
            em.wait("tensor", "act")
            for ki in range(16):
                slot = ki % 2
                em.wait("sync", "pewc", max(0, em.mark("pewc") - 1))
                em.dma("sync", wcol[slot][:, 0:512], wv_sc[vs, ki],
                       f"wld{slot}")
                em.wait("tensor", f"wld{slot}")
                for tc in range(4):
                    MM(pb[tc][:],
                       xmix[1][:, ki * 512 + tc * 128:ki * 512 + tc * 128 + 128],
                       wcol[slot][:, 0:512], ki == 0, ki == 15,
                       inc="pewc" if tc == 3 else None)
            em.wait("scalar", "pewc")
            for tc in range(4):
                ACT(vtok[:, tc * 2048 + vs * 512:tc * 2048 + vs * 512 + 512],
                    pb[tc][:], AF.Copy, inc="act")
        mus(4, xmix[0])

        if CUT == "a2":
            return cut_out()
        zbc = col(ZRO).broadcast_to([128, 512])
        em.fence(("vector", "gpsimd", "scalar", "tensor"),
                 ("dve", "act", "pool", "pe"))

        for p in range(8):
            em.op("vector", lambda e, p=p: e.tensor_tensor_scan(
                pc(p512, p), zbc, pc(wexp, p), 0.0, ALU.add, ALU.add),
                inc="dve")
        if CUT == "a3scan":
            return cut_out()
        em.fence(("gpsimd",), ("dve",))
        for p in range(8):
            CP("gpsimd", psi_f[:, p * 18:p * 18 + 16],
               p512[:, p * 512 + 15:p * 512 + 512:32], inc="pool")
            CP("gpsimd", psi_f[:, p * 18 + 16:p * 18 + 17],
               p512[:, p * 512 + 511:p * 512 + 512], inc="pool")
        em.fence(("vector", "scalar"), ("pool",))
        p3a = psi_f[:].rearrange("p (a c) -> p a c", a=8)
        TT("vector", dm1[:, 0:128].rearrange("p (a c) -> p a c", a=8),
           p3a[:, :, 0:16], p3a[:, :, 1:17], ALU.subtract, inc="dve")
        em.wait("scalar", "dve")
        for p in range(8):
            ACT(psi_b[:, p * 18:p * 18 + 16], dm1[:, p * 16:p * 16 + 16],
                AF.Exp, inc="act")
            ACT(psi_b[:, p * 18 + 17:p * 18 + 18],
                psi_f[:, p * 18:p * 18 + 1], AF.Exp, scale=-1.0, inc="act")
        if CUT == "a3psi":
            return cut_out()
        for p in range(8):
            mid = p512[:, p * 512 + 15:p * 512 + 512:32]
            midb = mid.unsqueeze(2).broadcast_to([128, 16, 32])
            p3 = pc(p512, p).rearrange("p (c j) -> p c j", c=16)
            em.wait("vector", "act" if p == 0 else "act")
            TT("vector", dm1[:].rearrange("p (c j) -> p c j", c=16), p3, midb,
               ALU.subtract, inc="dve")
            em.wait("scalar", "dve")
            ACT(pc(fk_s, p), dm1[:], AF.Exp, inc="act")
            em.wait("vector", "act")
            TT("vector", dm2[:], pc(wexp, p), dm1[:], ALU.subtract, inc="dve")
            em.wait("scalar", "dve")
            ACT(pc(fr_s, p), dm2[:], AF.Exp, inc="act")
        em.fence(("vector", "gpsimd"), ("act",))
        for p in range(8):
            TT("vector", pc(rt_s, p), pc(r_raw, p), pc(fr_s, p), ALU.mult,
               inc="dve")
            TT("vector", pc(kt_s, p), pc(k_raw, p), pc(fk_s, p), ALU.mult,
               inc="dve")
            TS("vector", pc(kb_s, p), pc(k_raw, p), col(BON + p), ALU.mult,
               inc="dve")
        em.fence(("vector",), ("dve",))
        for p in range(8):
            psv = psi_b[:, p * 18:p * 18 + 16].unsqueeze(2)
            psv = psv.broadcast_to([128, 16, 32])
            TT("vector", pc(khfm, p).rearrange("p (c j) -> p c j", c=16),
               pc(kt_s, p).rearrange("p (c j) -> p c j", c=16), psv, ALU.mult,
               inc="dve")
        if CUT == "a3fac":
            return cut_out()
        em.fence(("tensor",), ("dve",))
        tevm = {}
        for ii in range(32):
            kc, tc = ii // 4, ii % 4
            half = ii % 2
            if ii - 1 in tevm:
                em.wait("tensor", "dve", tevm[ii - 1])
            TRANS(tp_ps[:, half * 128:half * 128 + 128],
                  khfm[:, kc * 512 + tc * 128:kc * 512 + tc * 128 + 128],
                  inc="pe")
            em.wait("vector", "pe")
            CP("vector",
               kht[:, tc * 1024 + kc * 128:tc * 1024 + kc * 128 + 128],
               tp_ps[:, half * 128:half * 128 + 128], inc="dve")
            tevm[ii] = em.mark("dve")
        if CUT == "a3tp":
            return cut_out()
        si = sidx[0]
        if seq_start:
            MEMSET("gpsimd", tstate[1 - si][:], 0.0, inc="pool")
        else:
            entv = psi_b[:, 17::18][:, 0:8]
            em.wait("gpsimd", "act")
            TT("gpsimd", tstate[1 - si][:].rearrange("p (a v) -> p a v", a=8),
               tstate[si][:].rearrange("p (a v) -> p a v", a=8),
               entv.unsqueeze(2).broadcast_to([128, 8, 64]), ALU.mult,
               inc="pool")
        sidx[0] = 1 - si
        if CUT == "a3ent":
            return cut_out()
        proj(wg_sc, "wg", 16, xmix[0], sg_s, mix_i=4, cap=GPN)

        dump("rraw", r_raw[:], [128, 8 * 512], ti)
        dump("kts", kt_s[:], [128, 8 * 512], ti)
        dump("rts", rt_s[:], [128, 8 * 512], ti)
        dump("kht", kht[:], [128, 4 * 1024], ti)
        dump("vtok", vtok[:], [128, 4 * 2048], ti)
        dump("sgs", sg_s[:], [128, 16 * 512], ti)
        dump("p512", p512[:], [128, 8 * 512], ti, dt=F32)
        dump("psib", psi_b[:], [128, 8 * 18], ti)
        if CUT == "a3":
            return cut_out()
        em.fence(("tensor", "vector", "gpsimd", "scalar"),
                 ("dve", "pool", "act", "pe"))
        msu = cbf_s[:, MSU:MSU + 32].unsqueeze(1).broadcast_to([128, 16, 32])
        msd = cbf_s[:, MSD:MSD + 32].unsqueeze(1).broadcast_to([128, 16, 32])
        cur = sidx[0]
        oev_dve = {}
        for blk in range(min(NBLK, max(SBLK, 1))):
            em.fence(("tensor",), ("dve", "pool", "act"))
            for sg in range(4):
                if sg > 0:
                    em.wait("tensor", "pe")
                strip = 32 * sg
                for h in range(sg, NH, 4):
                    pchk = h // 4
                    for ci in range(4):
                        c = blk * 4 + ci
                        co = pchk * 512 + c * 32
                        lk = kt_s[strip:strip + 32, co:co + 32]
                        lb = kb_s[strip:strip + 32, co:co + 32]
                        rr = rt_s[strip:strip + 32, co:co + 32]
                        rw = r_raw[strip:strip + 32, co:co + 32]
                        colo = 32 * (h % 16)
                        MM(pb[h // 16][32 * ci:32 * ci + 32, colo:colo + 32],
                           lk, rr, True, True, tp=(strip, 32 * ci))
                        MM(pb[2 + h // 16][32 * ci:32 * ci + 32,
                                           colo:colo + 32],
                           lb, rw, True, True, tp=(strip, 32 * ci),
                           inc="pe" if (h >= NH - 4 and ci == 3) else None)
            em.fence(("vector",), ("pe",))
            for half in range(2):
                em.wait("vector", "pool", em.mark("pool"))
                TT("vector", dmb1[:].rearrange("p (a j) -> p a j", a=16),
                   pb[half][:].rearrange("p (a j) -> p a j", a=16), msu,
                   ALU.mult, inc="dve")
                TT("vector", dmb2[:].rearrange("p (a j) -> p a j", a=16),
                   pb[2 + half][:].rearrange("p (a j) -> p a j", a=16), msd,
                   ALU.mult, inc="dve")
                em.wait("gpsimd", "dve")
                TT("gpsimd", st_s[:, half * 512:half * 512 + 512], dmb1[:],
                   dmb2[:], ALU.add, inc="pool")
            if blk == 0:
                dump("sts0", st_s[:], [128, 1024], ti)
            if SPART == "sc":
                continue
            em.fence(("tensor",), ("pool",))
            for ci in range(4):
                c = blk * 4 + ci
                tc, tst = c // 4, 32 * (c % 4)
                sprev = tstate[cur]
                ibank = pb[ci]
                ebank = pb[4 + (ci % 2)]
                if ci >= 2:
                    em.wait("tensor", "dve", oev_dve.get(ci - 2, 0))
                for h in range(NH):
                    colo = (h // 2) * 32
                    lV = vtok[tst:tst + 32,
                              tc * 2048 + h * 64:tc * 2048 + h * 64 + 64]
                    rS = st_s[tst:tst + 32,
                              (h // 16) * 512 + 32 * (h % 16):
                              (h // 16) * 512 + 32 * (h % 16) + 32]
                    MM(ibank[64 * (h % 2):64 * (h % 2) + 64, colo:colo + 32],
                       lV, rS, True, True, tp=(tst, 64 * (h % 2)),
                       inc="pe" if h == NH - 1 else None)
                if SPART == "oi":
                    continue
                for sg in range(4):
                    em.wait("tensor", "pe")
                    strip = 32 * sg
                    for h in range(sg, NH, 4):
                        colo = (h // 2) * 32
                        lT = sprev[strip:strip + 32,
                                   (h // 4) * 64:(h // 4) * 64 + 64]
                        rr = rt_s[strip:strip + 32,
                                  (h // 4) * 512 + c * 32:
                                  (h // 4) * 512 + c * 32 + 32]
                        MM(ebank[64 * (h % 2):64 * (h % 2) + 64,
                                 colo:colo + 32],
                           lT, rr, True, True, tp=(strip, 64 * (h % 2)),
                           inc="pe" if h >= NH - 4 else None)
                em.wait("tensor", "pe")
                for h in range(NH):
                    strip = 32 * (h % 4)
                    lkh = kht[tst:tst + 32,
                              tc * 1024 + h * 32:tc * 1024 + h * 32 + 32]
                    rV = vtok[tst:tst + 32,
                              tc * 2048 + h * 64:tc * 2048 + h * 64 + 64]
                    MM(pb[6][strip:strip + 32,
                             (h // 4) * 64:(h // 4) * 64 + 64],
                       lkh, rV, True, True, tp=(tst, strip),
                       inc="pe" if h == NH - 1 else None)
                if SPART == "os":
                    continue
                em.fence(("gpsimd", "vector", "scalar"), ("pe",))
                psv = psi_b[:, c::18][:, 0:8].unsqueeze(2)
                TT("gpsimd", dmb3[:].rearrange("p (a v) -> p a v", a=8),
                   sprev[:].rearrange("p (a v) -> p a v", a=8),
                   psv.broadcast_to([128, 8, 64]), ALU.mult, inc="pool")
                em.wait("vector", "pool")
                TT("vector", tstate[1 - cur][:], dmb3[:], pb[6][:], ALU.add,
                   inc="dve")
                cur = 1 - cur
                em.fence(("tensor",), ("dve",))
                sq = dmb4 if ci % 2 == 0 else dmb5
                ACT(sq[:], ibank[:], AF.Copy, inc="act")
                em.wait("vector", "act")
                ob3c = ob16[:].rearrange("p (v c) -> p v c", v=16)
                TT("vector",
                   ob3c[:, :, blk * 128 + 32 * ci:blk * 128 + 32 * ci + 32],
                   sq[:].rearrange("p (v c) -> p v c", v=16), ebank[:]
                   .rearrange("p (v c) -> p v c", v=16), ALU.add, inc="dve")
                oev_dve[ci] = em.mark("dve")
        if SBLK < 99 or SPART:
            return cut_out()
        dump("ob16", ob16[:], [128, 16 * 512], ti)
        em.fence(("tensor",), ("act", "dve", "pool"))
        for vc in range(16):
            ep = cbf_s[:, EPA + 32 * vc:EPA + 32 * vc + 32]
            MM(pb[0][0:32, :], ep, pc(ob16, vc), vc == 0, vc == 15,
               inc="pe" if vc == 15 else None)
        em.wait("scalar", "pe")
        ACT(gnm_s[:], pb[0][0:32, :], AF.Copy, scale=1.0 / HV, inc="act")
        em.fence(("tensor",), ("act",))
        ocm = {}
        for vc in range(16):
            em.wait("tensor", "dve", max(0, em.mark("dve") - 3))
            MM(pb[2 + vc % 4][:], gpat_s[0:32, vc * 128:vc * 128 + 128],
               gnm_s[:], True, True, inc="pe")
            em.wait("vector", "pe")
            TT("vector", pc(oc_b, vc), pc(ob16, vc), pb[2 + vc % 4][:],
               ALU.subtract, inc="dve")
            ocm[vc] = em.mark("dve")
        for vc in range(16):
            sq = dmb3 if vc % 2 == 0 else dmb2
            em.wait("scalar", "dve", ocm[vc])
            em.wait("scalar", "pe", max(0, em.mark("pe") - 1))
            ACT(sq[:], pc(oc_b, vc), AF.Square, inc="act")
            em.wait("tensor", "act")
            MM(pb[1][0:32, :], cbf_s[:, EPA + 32 * vc:EPA + 32 * vc + 32],
               sq[:], vc == 0, vc == 15, inc="pe")
        em.wait("scalar", "pe")
        ACT(gnscr[:], pb[1][0:32, :], AF.Ln, bias=cols_s[0:32, EPSC:EPSC + 1],
            scale=1.0 / HV, inc="act")
        ACT(gnr_s[:], gnscr[:], AF.Exp, scale=-0.5, inc="act")
        em.fence(("tensor",), ("act",))
        for vc in range(16):
            sil = dmb4 if vc % 2 == 0 else dmb5
            mid = dmb1 if vc % 2 == 0 else dmb3
            em.wait("tensor", "dve", max(0, em.mark("dve") - 3))
            MM(pb[2 + vc % 4][:], gpat_s[0:32, vc * 128:vc * 128 + 128],
               gnr_s[:], True, True, inc="pe")
            em.wait("scalar", "dve", max(0, em.mark("dve") - 1))
            ACT(sil[:], pc(sg_s, vc), AF.Silu, inc="act")
            em.wait("vector", "pe")
            em.wait("vector", "act")
            STT("vector", mid[:], pc(oc_b, vc), col(GNW), pb[2 + vc % 4][:],
                ALU.mult, ALU.mult, inc="dve")
            STT("vector", pc(oprime, vc), mid[:], col(GNB), sil[:],
                ALU.add, ALU.mult, inc="dve")
        dump("oprime", oprime[:], [128, 16 * 512], ti)

        if CUT == "a4":
            return cut_out()
        em.fence(("vector", "gpsimd", "scalar", "tensor"),
                 ("dve", "act", "pool", "pe"))
        wait_ready("sync", "wo")
        em.fence(("tensor", "sync"), ("dve",))
        src = x_d[b, :, t0:t0 + NT].rearrange("(n p) c -> p n c", p=128)
        em.dma("gpsimd", xrel[:].rearrange("p (n c) -> p n c", n=16), src,
               "xld2")
        evm5 = {}
        em.fence(("tensor",), ("act", "dve"))
        for m in range(16):
            ch = f"wld{m % 2}"
            em.wait("sync", "pewc", max(0, em.mark("pewc") - 1))
            em.dma("sync", wcol[m % 2][:].rearrange("p (k c) -> p k c", k=16),
                   wo_sc[m].rearrange("k p c -> p k c"), ch)
            em.wait("tensor", ch)
            bank = 4 + (m % 2)
            if m - 2 in evm5:
                em.wait("tensor", "act", evm5[m - 2])
            for ki in range(16):
                MM(pb[bank][:], wcol[m % 2][:, ki * 128:ki * 128 + 128],
                   pc(oprime, ki), ki == 0, ki == 15,
                   inc="pewc" if ki == 15 else None)
            em.wait("scalar", "pewc")
            ACT(pc(ao_b, m), pb[bank][:], AF.Copy, inc="act")
            evm5[m] = em.mark("act")
        em.fence(("tensor", "vector", "scalar"), ("act",))
        em.fence(("vector", "gpsimd"), ("xld2",))

        def ln2_write(p):
            em.wait("vector", "pool")
            TT("vector", dmb2[:], dmb1[:], pb[3][:], ALU.mult, inc="dve")
            em.wait("gpsimd", "dve")
            TT("gpsimd", pc(xprime, p), pc(xrel, p), dmb2[:], ALU.add,
               inc="pool")
        ln_stats_apply(ao_b, ln2_write)
        dump("xprime", xprime[:], [128, 16 * 512], ti, dt=F32)

        if CUT == "a5":
            return cut_out()
        em.fence(("vector", "gpsimd", "scalar", "tensor"),
                 ("dve", "act", "pool", "pe"))

        for p in range(16):
            ACT(pc(xln3, p), pc(xprime, p), AF.Copy, inc="act")
        em.fence(("tensor", "vector"), ("act",))

        def ln3_write(p):
            TT("vector", pc(xln3, p), dmb1[:], pb[3][:], ALU.mult, inc="dve")
        ln_stats_apply(xln3, ln3_write)
        wait_ready("sync", "gt")
        wait_ready("sync", "up")
        em.fence(("tensor", "sync"), ("dve",))
        gum = {}
        gam = {}
        em.fence(("tensor",), ("act", "dve"))
        for m in range(44):
            ch = f"wld{m % 2}"
            if m - 2 in gum:
                em.wait("sync", "dve", gum[m - 2])
            em.dma("sync", gcol[m % 2][:].rearrange("p (k c) -> p k c", k=16),
                   gt_sc[m].rearrange("k p c -> p k c"), ch)
            em.dma("sync", ucol[m % 2][:].rearrange("p (k c) -> p k c", k=16),
                   up_sc[m].rearrange("k p c -> p k c"), ch)
            em.wait("tensor", ch)
            gb = pb[4] if m % 2 == 0 else pb[6]
            ub = pb[5] if m % 2 == 0 else pb[0]
            if m - 2 in gum:
                em.wait("tensor", "dve", gum[m - 2])
            if m - 2 in gam:
                em.wait("tensor", "act", gam[m - 2])
            for ki in range(16):
                MM(gb[:], gcol[m % 2][:, ki * 128:ki * 128 + 128],
                   pc(xln3, ki), ki == 0, ki == 15)
                MM(ub[:], ucol[m % 2][:, ki * 128:ki * 128 + 128],
                   pc(xln3, ki), ki == 0, ki == 15,
                   inc="pe" if ki == 15 else None)
            em.wait("scalar", "pe")
            sil = dmb1 if m % 2 == 0 else dmb2
            if m - 2 in gum:
                em.wait("scalar", "dve", gum[m - 2])
            ACT(sil[:], gb[:], AF.Silu, scale=col(SGA + m), inc="act")
            gam[m] = em.mark("act")
            em.wait("vector", "act")
            em.op("vector", lambda e, m=m, ub=ub, sil=sil: e.scalar_tensor_tensor(
                pc(gu_s, m), ub[:], col(SUP + m), sil[:], ALU.mult, ALU.mult))
            gum[m] = em.mark("dve")

        dump("xln3", xln3[:], [128, 16 * 512], ti)
        dump("gus", gu_s[:], [128, 44 * 512], ti)
        if CUT == "f1":
            return cut_out()
        em.fence(("vector", "gpsimd", "scalar", "tensor"),
                 ("dve", "act", "pool", "pe"))
        wait_ready("sync", "dn")
        em.fence(("tensor", "sync"), ("dve",))
        evm6 = {}
        em.fence(("tensor",), ("act",))
        for m in range(16):
            ch = f"wld{m % 2}"
            em.wait("sync", "pewc", max(0, em.mark("pewc") - 1))
            em.dma("sync", dcol[m % 2][:].rearrange("p (k c) -> p k c", k=44),
                   dn_sc[m].rearrange("k p c -> p k c"), ch)
            em.wait("tensor", ch)
            bank = 4 + (m % 2)
            if m - 2 in evm6:
                em.wait("tensor", "act", evm6[m - 2])
            for ki in range(44):
                MM(pb[bank][:], dcol[m % 2][:, ki * 128:ki * 128 + 128],
                   pc(gu_s, ki), ki == 0, ki == 43,
                   inc="pewc" if ki == 43 else None)
            em.wait("scalar", "pewc")
            ACT(pc(dnb, m), pb[bank][:], AF.Copy, scale=col(SDN + m),
                inc="act")
            evm6[m] = em.mark("act")
        em.fence(("tensor", "vector", "scalar"), ("act",))

        dump("dnb", dnb[:], [128, 16 * 512], ti)

        def ln4_write(p):
            em.wait("vector", "pool")
            TT("vector", dmb2[:], dmb1[:], pb[3][:], ALU.mult, inc="dve")
            em.wait("gpsimd", "dve")
            TT("gpsimd", pc(y_s, p), pc(xprime, p), dmb2[:], ALU.add,
               inc="pool")
        ln_stats_apply(dnb, ln4_write)
        em.fence(("gpsimd",), ("pool",))
        dst = y_d[b, :, t0:t0 + NT].rearrange("(n p) c -> p n c", p=128)
        em.dma("gpsimd", dst, y_s[:].rearrange("p (n c) -> p n c", n=16), "st")

    if CUT == "pre":
        em.fence(("gpsimd",), ("dve", "act", "pool", "pe", "pldc"))
        for chn, v0, ch1, v1 in [pre_ready[k] for k in pre_ready]:
            em.wait("gpsimd", chn, v0)
            em.wait("gpsimd", ch1, v1)
        em.dma("gpsimd", y_d[0, 0:128, 0:512], dm1[:], "st")
    else:
        for ti in range(n_tiles):
            tile(ti)

    em.wait("gpsimd", "st")
    em.wait("sync", "st")
    em.emit()
    return nc


def _host_inputs(inputs):
    x = np.ascontiguousarray(inputs["L0_attn_hidden_states"])
    mu_x = np.asarray(inputs["L0_attn_mu_x"], np.float32)
    Wx1 = np.asarray(inputs["L0_attn_Wx1"], np.float32)
    Wx2 = np.asarray(inputs["L0_attn_Wx2"], np.float32)
    Wr = np.ascontiguousarray(inputs["L0_attn_Wr"], np.float32)
    Ww1 = np.asarray(inputs["L0_attn_Ww1"], np.float32)
    Ww2 = np.asarray(inputs["L0_attn_Ww2"], np.float32)
    bw = np.asarray(inputs["L0_attn_bw"], np.float32)
    Wk = np.ascontiguousarray(inputs["L0_attn_Wk"], np.float32)
    Wv = np.ascontiguousarray(inputs["L0_attn_Wv"], np.float32)
    Wg = np.ascontiguousarray(inputs["L0_attn_Wg"], np.float32)
    bonus = np.asarray(inputs["L0_attn_bonus"], np.float32)
    gn_w = np.asarray(inputs["L0_attn_gn_w"], np.float32)
    gn_b = np.asarray(inputs["L0_attn_gn_b"], np.float32)
    Wo = np.ascontiguousarray(inputs["L0_attn_Wo"], np.float32)
    wqg = np.asarray(inputs["L0_ffn_wq_gate"], np.int32)
    sgate = np.asarray(inputs["L0_ffn_s_gate"], np.float32)
    wqu = np.asarray(inputs["L0_ffn_wq_up"], np.int32)
    sup = np.asarray(inputs["L0_ffn_s_up"], np.float32)
    wqd = np.asarray(inputs["L0_ffn_wq_down"], np.int32)
    sdown = np.asarray(inputs["L0_ffn_s_down"], np.float32)

    wx1_l = np.ascontiguousarray(
        Wx1.reshape(16, 128, 160).transpose(1, 0, 2).reshape(128, 16 * 160))
    wx2a_l = np.ascontiguousarray(Wx2[0:4].reshape(128, D))
    wx2b_l = np.ascontiguousarray(Wx2[4])
    ww1_l = np.ascontiguousarray(
        Ww1.reshape(16, 128, 64).transpose(1, 0, 2).reshape(128, 16 * 64))
    ww2_l = np.ascontiguousarray(np.concatenate([Ww2, bw[None, :]], axis=0))
    wqg_t = np.ascontiguousarray(wqg.T)
    wqu_t = np.ascontiguousarray(wqu.T)
    wqd_t = np.ascontiguousarray(wqd.T)

    cols = np.zeros((128, NCOLS), np.float32)
    cols[:, BON:BON + 8] = bonus.reshape(-1).reshape(8, 128).T
    cols[:, MUX:MUX + 16] = mu_x.reshape(16, 128).T
    cols[:, GNW] = np.tile(gn_w, 2)
    cols[:, GNB] = np.tile(gn_b, 2)
    cols[:, SGA:SGA + 44] = sgate.reshape(44, 128).T
    cols[:, SUP:SUP + 44] = sup.reshape(44, 128).T
    cols[:, SDN:SDN + 16] = sdown.reshape(16, 128).T
    cols[:, EPSC] = EPS

    cbf = np.zeros((128, NBF), np.float32)
    j = np.arange(32)
    m32 = (j[:, None] < j[None, :]).astype(np.float32)
    d32 = np.eye(32, dtype=np.float32)
    cbf[:, MSU:MSU + 32] = np.tile(m32, (4, 1))
    cbf[:, MSD:MSD + 32] = np.tile(d32, (4, 1))
    cbf[:, IDE:IDE + 128] = np.eye(128, dtype=np.float32)
    for vc in range(16):
        ep = np.zeros((128, 32), np.float32)
        ep[0:64, 2 * vc] = 1.0
        ep[64:128, 2 * vc + 1] = 1.0
        cbf[:, EPA + 32 * vc:EPA + 32 * vc + 32] = ep
    cbf[:, ONE] = 1.0
    cbf[:, ONE + 1] = 1.0
    cbf_b = cbf.astype(bf16)

    gpat = np.zeros((32, 2048), np.float32)
    for pcn in range(16):
        gp = np.zeros((32, 128), np.float32)
        gp[2 * pcn, 0:64] = 1.0
        gp[2 * pcn + 1, 64:128] = 1.0
        gpat[:, 128 * pcn:128 * pcn + 128] = gp
    gpat_b = gpat.astype(bf16)

    shared = dict(wx1=wx1_l, wx2a=wx2a_l, wx2b=wx2b_l, ww1=ww1_l,
                  ww2aug=ww2_l, wr=Wr, wk=Wk, wv=Wv, wg=Wg, wo=Wo,
                  wqg=wqg_t, wqu=wqu_t, wqd=wqd_t, cols=cols, cbf=cbf_b,
                  gpat=gpat_b)
    maps = []
    for c in range(8):
        xt = np.ascontiguousarray(x[2 * c:2 * c + 2].transpose(0, 2, 1))
        m = dict(shared)
        m["x"] = xt
        maps.append(m)
    return maps


def kernel(**inputs):
    nc = build(N_TILES)
    in_maps = _host_inputs(inputs)
    res = run_bass_kernel_spmd(nc, in_maps, core_ids=list(range(8)))
    y = np.empty((16, LSEQ, D), np.float32)
    for c in range(8):
        yc = np.asarray(res.results[c]["y"])
        y[2 * c:2 * c + 2] = yc.transpose(0, 2, 1)
    return y


# version 02
# speedup vs baseline: 1.1443x; 1.1443x over previous
import os
import numpy as np
import ml_dtypes
import concourse.bass as bass
import concourse.mybir as mybir
from concourse.bass_utils import run_bass_kernel_spmd

F32 = mybir.dt.float32
BF16 = mybir.dt.bfloat16
I32 = mybir.dt.int32
AF = mybir.ActivationFunctionType
ALU = mybir.AluOpType
bf16 = ml_dtypes.bfloat16

D = 2048
LSEQ = 2048
NT = 512
NH, HQK, HV = 32, 32, 64
DK, DV, FF = 1024, 2048, 5632
CH = 32
NCH = NT // CH
NBLK = NT // 128
EPS = 1e-5

PRE, A1, A2, A3, A4, A5, F1, F2 = range(8)

N_TILES = int(os.environ.get("MK_NTILES", "0")) or 8
DEBUG = os.environ.get("MK_DEBUG", "")
CUT = os.environ.get("MK_CUT", "")
TPV = os.environ.get("MK_TPV", "")
TPN = int(os.environ.get("MK_TPN", "32"))
GPN = int(os.environ.get("MK_GPN", "16"))
GDST = os.environ.get("MK_GDST", "")
SBLK = int(os.environ.get("MK_SBLK", "99"))
SPART = os.environ.get("MK_SPART", "")

BON = 0
MUX = 8
GNW = 24
GNB = 25
SGA = 26
SUP = 70
SDN = 114
EPSC = 130
ZRO = 131
NCOLS = 132
MSU = 0
MSD = 32
IDE = 64
EPA = 192
ONE = 704
NBF = 706


class Em:
    ENGINES = ("sync", "gpsimd", "vector", "scalar", "tensor")

    def __init__(self, nc):
        self.nc = nc
        self.items = {e: [] for e in self.ENGINES}
        self.sems = {}
        self.cnt = {}
        self.last_wait = {e: {} for e in self.ENGINES}

    def _sem(self, name):
        if name not in self.sems:
            self.sems[name] = self.nc.alloc_semaphore(name)
            self.cnt[name] = 0
        return self.sems[name]

    def mark(self, ch):
        self._sem(ch)
        return self.cnt[ch]

    def wait(self, eng, ch, val=None):
        self._sem(ch)
        v = self.cnt[ch] if val is None else val
        if v <= 0 or self.last_wait[eng].get(ch, -1) >= v:
            return
        self.last_wait[eng][ch] = v
        self.items[eng].append(("w", ch, v))

    ECH = {"vector": "dve", "scalar": "act", "gpsimd": "pool"}

    def op(self, eng, fn, inc=None, inc_n=1, dma=False):
        ch = self.ECH.get(eng)
        if ch is not None and not dma:
            assert inc is None or inc == ch, (eng, inc)
            self.wait(eng, ch)
            inc, inc_n = ch, 1
        if inc:
            self._sem(inc)
            self.cnt[inc] += inc_n
        self.items[eng].append(("o", fn, inc, inc_n))

    def dma(self, eng, out, in_, inc):
        self.op(eng, lambda e, o=out, i=in_: e.dma_start(out=o, in_=i), inc, 16,
                dma=True)

    def fence(self, consumers, producers):
        for eng in consumers:
            for ch in producers:
                self.wait(eng, ch)

    def emit(self):
        with self.nc.Block() as block:
            for eng in self.ENGINES:
                items = self.items[eng]
                if not items:
                    continue

                def mk(items=items):
                    def body(e):
                        for it in items:
                            if it[0] == "w":
                                e.wait_ge(self.sems[it[1]], it[2])
                            else:
                                ins = it[1](e)
                                if it[2] is not None:
                                    ins.then_inc(self.sems[it[2]], it[3])
                    return body
                getattr(block, eng)(mk())


class Plan:
    def __init__(self, nc, base, size):
        self.nc = nc
        self.base = base
        self.size = size
        self.items = []

    def tensor(self, name, shape, dtype, s, e, align=128):
        isz = 2 if dtype == BF16 else 4
        nb = int(np.prod(shape[1:])) * isz if len(shape) > 1 else isz
        nb = (nb + align - 1) // align * align
        spans = sorted((it[1], it[1] + it[2]) for it in self.items
                       if not (e < it[3] or s > it[4]))
        off = 0
        for a, b in spans:
            if off + nb <= a:
                break
            off = max(off, b)
        assert off + nb <= self.size, \
            f"sbuf overflow: {name} needs {nb} at {off}, cap {self.size}"
        self.items.append((name, off, nb, s, e))
        return self.nc.alloc_sbuf_tensor_at(name, list(shape), dtype,
                                            offset=self.base + off)


def build(n_tiles=8):
    nc = bass.Bass("TRN2", target_bir_lowering=False)

    def din(name, shape, dt=F32):
        return nc.dram_tensor(name, list(shape), dt, kind="ExternalInput").ap()

    x_d = din("x", [2, D, LSEQ])
    wx1_d = din("wx1", [128, 16 * 160])
    wx2a_d = din("wx2a", [128, D])
    wx2b_d = din("wx2b", [32, D])
    ww1_d = din("ww1", [128, 16 * 64])
    ww2_d = din("ww2aug", [65, DK])
    wr_d = din("wr", [D, DK])
    wk_d = din("wk", [D, DK])
    wv_d = din("wv", [D, DV])
    wg_d = din("wg", [D, DV])
    wo_d = din("wo", [DV, D])
    wqg_d = din("wqg", [D, FF], I32)
    wqu_d = din("wqu", [D, FF], I32)
    wqd_d = din("wqd", [FF, D], I32)
    cols_d = din("cols", [128, NCOLS])
    cbf_d = nc.dram_tensor("cbf", [128, NBF], BF16, kind="ExternalInput").ap()
    gpat_d = nc.dram_tensor("gpat", [32, 2048], BF16, kind="ExternalInput").ap()

    y_d = nc.dram_tensor("y", [2, D, LSEQ], F32, kind="ExternalOutput").ap()

    def scr(name, shape):
        return nc.dram_tensor(name, list(shape), BF16, kind="Internal").ap()

    wr_sc = scr("wr_sc", [8, 16, 128, 128])
    wk_sc = scr("wk_sc", [8, 16, 128, 128])
    wv_sc = scr("wv_sc", [4, 16, 128, 512])
    wg_sc = scr("wg_sc", [16, 16, 128, 128])
    wo_sc = scr("wo_sc", [16, 16, 128, 128])
    gt_sc = scr("gt_sc", [44, 16, 128, 128])
    up_sc = scr("up_sc", [44, 16, 128, 128])
    dn_sc = scr("dn_sc", [16, 44, 128, 128])

    cap = nc.sbuf_bytes_remaining
    lo, hi = nc.bump_sbuf(cap - 256)
    pl = Plan(nc, lo, hi - lo)
    P0, P1 = PRE, F2

    wx1_s = pl.tensor("wx1_s", [128, 16 * 160], BF16, P0, P1)
    wx2a_s = pl.tensor("wx2a_s", [128, D], BF16, P0, P1)
    wx2b_s = pl.tensor("wx2b_s", [32, D], BF16, P0, P1)
    ww1_s = pl.tensor("ww1_s", [128, 16 * 64], BF16, P0, P1)
    ww2_s = pl.tensor("ww2_s", [65, DK], BF16, P0, P1)
    cols_s = pl.tensor("cols_s", [128, NCOLS], F32, P0, P1)
    cbf_s = pl.tensor("cbf_s", [128, NBF], BF16, P0, P1)
    gpat_s = pl.tensor("gpat_s", [32, 2048], BF16, P0, P1)
    tstate = [pl.tensor(f"tstate{i}", [128, 512], BF16, P0, P1)
              for i in range(2)]
    psi_f = pl.tensor("psi_f", [128, 8 * 18], F32, A3, A4)
    psi_b = pl.tensor("psi_b", [128, 8 * 18], BF16, A3, A4)
    zb0 = pl.tensor("zb0", [128, NT], BF16, A2, A2)
    zb1 = pl.tensor("zb1", [32, NT], BF16, A2, A2)
    tanha = pl.tensor("tanha", [65, NT], BF16, A2, A2)
    rows_b = pl.tensor("rows_b", [1, 1024], BF16, P0, P1)
    rows_f = pl.tensor("rows_f", [1, 1024], F32, P0, P1)
    gnm_s = pl.tensor("gnm_s", [32, 512], BF16, A4, A4)
    gnr_s = pl.tensor("gnr_s", [32, 512], BF16, A4, A4)
    gnscr = pl.tensor("gnscr", [32, 512], F32, A4, A4)
    dm1 = pl.tensor("dm1", [128, 512], F32, P0, P1)
    dm2 = pl.tensor("dm2", [128, 512], F32, P0, P1)
    dmb1 = pl.tensor("dmb1", [128, 512], BF16, P0, P1)
    dmb2 = pl.tensor("dmb2", [128, 512], BF16, P0, P1)
    dmb3 = pl.tensor("dmb3", [128, 512], BF16, P0, P1)
    dmb4 = pl.tensor("dmb4", [128, 512], BF16, P0, P1)
    dmb5 = pl.tensor("dmb5", [128, 512], BF16, P0, P1)
    carry_s = pl.tensor("carry_s", [128, 16], BF16, P0, P1)

    xin = pl.tensor("xin", [128, 16 * 512], F32, A1, A1)
    xb16 = pl.tensor("xb16", [128, 16 * 512], BF16, A1, A1)
    xln = pl.tensor("xln", [128, 16 * 513], BF16, A1, A2)
    delta = pl.tensor("delta", [128, 16 * 512], BF16, A2, A2)
    tmp_b = pl.tensor("tmp_b", [128, 16 * 512], BF16, A2, A2)
    xmix = [pl.tensor(f"xmix{i}", [128, 16 * 512], BF16, A2, A3 if i == 0 else A2)
            for i in range(2)]
    vtok = pl.tensor("vtok", [128, 4 * 2048], BF16, A2, A4)
    sg_s = pl.tensor("sg_s", [128, 16 * 512], BF16, A3, A4)
    r_raw = pl.tensor("r_raw", [128, 8 * 512], BF16, A2, A4)
    k_raw = pl.tensor("k_raw", [128, 8 * 512], BF16, A2, A3)
    wexp = pl.tensor("wexp", [128, 8 * 512], F32, A2, A3)
    p512 = pl.tensor("p512", [128, 8 * 512], F32, A3, A3)
    fk_s = pl.tensor("fk_s", [128, 8 * 512], BF16, A3, A3)
    fr_s = pl.tensor("fr_s", [128, 8 * 512], BF16, A3, A3)
    khfm = pl.tensor("khfm", [128, 8 * 512], BF16, A3, A3)
    rt_s = pl.tensor("rt_s", [128, 8 * 512], BF16, A3, A4)
    kt_s = pl.tensor("kt_s", [128, 8 * 512], BF16, A3, A4)
    kht = pl.tensor("kht", [128, 4 * 1024], BF16, A3, A4)
    kb_s = pl.tensor("kb_s", [128, 8 * 512], BF16, A3, A4)
    st_s = pl.tensor("st_s", [128, 1024], BF16, A4, A4)
    ob16 = pl.tensor("ob16", [128, 16 * 512], BF16, A4, A4)
    oc_b = pl.tensor("oc_b", [128, 16 * 512], BF16, A4, A4)
    oprime = pl.tensor("oprime", [128, 16 * 512], BF16, A4, A5)
    ao_b = pl.tensor("ao_b", [128, 16 * 512], BF16, A5, A5)
    xrel = pl.tensor("xrel", [128, 16 * 512], F32, A5, A5)
    xprime = pl.tensor("xprime", [128, 16 * 512], F32, A5, F2)
    xln3 = pl.tensor("xln3", [128, 16 * 512], BF16, F1, F1)
    gu_s = pl.tensor("gu_s", [128, 44 * 512], BF16, F1, F2)
    dnb = pl.tensor("dnb", [128, 16 * 512], BF16, F2, F2)
    y_s = pl.tensor("y_s", [128, 16 * 512], F32, F2, F2)
    wcol = [pl.tensor(f"wcol{i}", [128, 2048], BF16, A2, A5) for i in range(2)]
    gcol = [pl.tensor(f"gcol{i}", [128, 2048], BF16, F1, F1) for i in range(2)]
    ucol = [pl.tensor(f"ucol{i}", [128, 2048], BF16, F1, F1) for i in range(2)]
    dcol = [pl.tensor(f"dcol{i}", [128, 44 * 128], BF16, F2, F2)
            for i in range(2)]
    pst_i = [pl.tensor(f"pst_i{i}", [128, FF], I32, PRE, PRE) for i in range(2)]
    pst_b = [pl.tensor(f"pst_b{i}", [128, FF], BF16, PRE, PRE)
             for i in range(2)]
    pst_f = [pl.tensor(f"pst_f{i}", [128, 2816], F32, PRE, PRE)
             for i in range(2)]
    pst_o = [pl.tensor(f"pst_o{i}", [128, 2816], BF16, PRE, PRE)
             for i in range(2)]

    tp_ps = nc.alloc_psum_tensor("tp_ps", [128, 1024], BF16)
    pb = [nc.alloc_psum_tensor(f"pb{i}", [128, 512], F32) for i in range(7)]

    em = Em(nc)

    def col(idx):
        return cols_s[:, idx:idx + 1]

    def TT(eng, out, a, b, op, inc=None):
        em.op(eng, lambda e, o=out, x=a, y=b, p=op: e.tensor_tensor(o, x, y, p),
              inc)

    def STT(eng, out, in0, scalar, in1, op0, op1, inc=None):
        em.op(eng, lambda e, o=out, i0=in0, s=scalar, i1=in1, p0=op0, p1=op1:
              e.scalar_tensor_tensor(o, i0, s, i1, p0, p1), inc)

    def TS(eng, out, in0, s1, op0, inc=None, s2=None, op1=None):
        if op1 is None:
            em.op(eng, lambda e, o=out, i=in0, s=s1, p=op0:
                  e.tensor_scalar(o, i, s, None, p), inc)
        else:
            em.op(eng, lambda e, o=out, i=in0, a=s1, b=s2, p0=op0, p1=op1:
                  e.tensor_scalar(o, i, a, b, p0, p1), inc)

    def ACT(out, in_, func, bias=0.0, scale=1.0, inc=None):
        em.op("scalar", lambda e, o=out, i=in_, f=func, b=bias, s=scale:
              e.activation(o, i, f, bias=b, scale=s), inc)

    def CP(eng, out, in_, inc=None):
        em.op(eng, lambda e, o=out, i=in_: e.tensor_copy(o, i), inc)

    def MM(out, lhsT, rhs, start, stop, tp=None, inc=None):
        em.op("tensor", lambda e, o=out, l=lhsT, r=rhs, s0=start, s1=stop, t=tp:
              e.matmul(o, l, r, start=s0, stop=s1, tile_position=t,
                       skip_group_check=True), inc)

    def TRANS(out, in_, inc=None):
        em.op("tensor", lambda e, o=out, i=in_:
              e.transpose(o, i, cbf_s[:, IDE:IDE + 128]), inc)

    def MEMSET(eng, ap, v, inc=None):
        em.op(eng, lambda e, a=ap, vv=v: e.memset(a, vv), inc)

    def pc(t, i, c0=0, c1=512, w=512):
        return t[:, i * w + c0: i * w + c1]

    small = [(wx1_d, wx1_s, 16 * 160, 128), (wx2a_d, wx2a_s, D, 128),
             (wx2b_d, wx2b_s, D, 32), (ww1_d, ww1_s, 16 * 64, 128),
             (ww2_d, ww2_s, DK, 65)]
    for i, (dsrc, sdst, ncols, nprt) in enumerate(small):
        stg = pst_f[i % 2]
        ch = f"pld{i % 2}"
        em.wait("sync", "dve", em.mark("dve"))
        em.dma("sync", stg[0:nprt, 0:ncols], dsrc, ch)
        em.wait("vector", ch)
        CP("vector", sdst[0:nprt, 0:ncols], stg[0:nprt, 0:ncols], inc="dve")
    em.dma("sync", cols_s[:], cols_d, "pldc")
    em.dma("sync", cbf_s[:], cbf_d, "pldc")
    em.dma("sync", gpat_s[:], gpat_d, "pldc")
    MEMSET("vector", tstate[0][:], 0.0, inc="dve")
    MEMSET("vector", tstate[1][:], 0.0, inc="dve")

    pre_ready = {}
    st_gate = {0: None, 1: None}

    def conv_attn(name, dsrc, dscr, nm, w=128):
        for ki in range(16):
            s = ki % 2
            stch = f"pst_{name}{s}"
            em.wait("sync", "dve", em.mark("dve") - (1 if ki >= 1 else 0))
            if ki == 0:
                em.wait("sync", "pool", em.mark("pool"))
            ncol = dsrc.shape[1]
            em.dma("sync", pst_f[s][:, 0:ncol],
                   dsrc[128 * ki:128 * ki + 128, :], f"pld{s}")
            em.wait("vector", f"pld{s}")
            if st_gate[s]:
                em.wait("vector", st_gate[s][0], st_gate[s][1])
            CP("vector", pst_o[s][:, 0:ncol], pst_f[s][:, 0:ncol], inc="dve")
            em.wait("scalar", "dve")
            nmm = ncol // w
            src = pst_o[s][:, 0:ncol].rearrange("p (m c) -> p m c", m=nmm)
            dst = dscr[:, ki, :, :].transpose([1, 0, 2])
            em.dma("scalar", dst, src, stch)
            st_gate[s] = (stch, em.mark(stch))
        pre_ready[name] = (f"pst_{name}0", em.mark(f"pst_{name}0"),
                           f"pst_{name}1", em.mark(f"pst_{name}1"))

    conv_attn("wr", wr_d, wr_sc, 8)
    conv_attn("wk", wk_d, wk_sc, 8)
    conv_attn("wv", wv_d, wv_sc, 4, w=512)
    conv_attn("wg", wg_d, wg_sc, 16)
    conv_attn("wo", wo_d, wo_sc, 16)

    def conv_ffn(name, dsrc, dscr, nk, nm):
        for ki in range(nk):
            s = ki % 2
            stch = f"pst_{name}{s}"
            em.wait("sync", "dve", em.mark("dve") - (1 if ki >= 1 else 0))
            em.dma("sync", pst_i[s][:, 0:nm * 128],
                   dsrc[128 * ki:128 * ki + 128, :], f"pld{s}")
            em.wait("vector", f"pld{s}")
            if st_gate[s]:
                em.wait("vector", st_gate[s][0], st_gate[s][1])
            TS("vector", pst_b[s][:, 0:nm * 128], pst_i[s][:, 0:nm * 128],
               -128.0, ALU.add, inc="dve")
            em.wait("scalar", "dve")
            src = pst_b[s][:, 0:nm * 128].rearrange("p (m c) -> p m c", m=nm)
            dst = dscr[:, ki, :, :].transpose([1, 0, 2])
            em.dma("scalar", dst, src, stch)
            st_gate[s] = (stch, em.mark(stch))
        pre_ready[name] = (f"pst_{name}0", em.mark(f"pst_{name}0"),
                           f"pst_{name}1", em.mark(f"pst_{name}1"))

    conv_ffn("gt", wqg_d, gt_sc, 16, 44)
    conv_ffn("up", wqu_d, up_sc, 16, 44)
    conv_ffn("dn", wqd_d, dn_sc, 44, 16)

    def wait_ready(eng, name):
        c0, v0, c1, v1 = pre_ready[name]
        em.wait(eng, c0, v0)
        em.wait(eng, c1, v1)

    def ln_stats_apply(cast_t, out_write):
        onecol = cbf_s[:, ONE:ONE + 1]
        onerow = cbf_s[0:1, ONE:ONE + 1].broadcast_to([1, 128])
        em.fence(("tensor",), ("dve", "pool", "act"))
        for p in range(16):
            MM(pb[0][0:1, :], onecol, pc(cast_t, p), p == 0, p == 15,
               inc="pe" if p == 15 else None)
        for p in range(16):
            sq = dmb3 if p % 2 == 0 else dmb2
            em.wait("scalar", "pe", max(0, em.mark("pe") - 1))
            ACT(sq[:], pc(cast_t, p), AF.Square, inc="act")
            em.wait("tensor", "act")
            MM(pb[1][0:1, :], onecol, sq[:], p == 0, p == 15, inc="pe")
        em.wait("scalar", "pe")
        ACT(rows_b[0:1, 0:512], pb[0][0:1, :], AF.Copy, scale=1.0 / D,
            inc="act")
        ACT(rows_f[0:1, 0:512], rows_b[0:1, 0:512], AF.Square, inc="act")
        em.wait("vector", "act")
        em.op("vector", lambda e: e.scalar_tensor_tensor(
            rows_f[0:1, 512:1024], pb[1][0:1, :], 1.0 / D,
            rows_f[0:1, 0:512], ALU.mult, ALU.subtract), inc="dve")
        em.wait("scalar", "dve")
        ACT(rows_f[0:1, 0:512], rows_f[0:1, 512:1024], AF.Ln,
            bias=cols_s[0:1, EPSC:EPSC + 1], inc="act")
        ACT(rows_b[0:1, 512:1024], rows_f[0:1, 0:512], AF.Exp, scale=-0.5,
            inc="act")
        em.wait("tensor", "act")
        MM(pb[2][:], onerow, rows_b[0:1, 0:512], True, True)
        MM(pb[3][:], onerow, rows_b[0:1, 512:1024], True, True, inc="pe")
        em.fence(("vector", "gpsimd"), ("pe",))
        for p in range(16):
            TT("vector", dmb1[:], pc(cast_t, p), pb[2][:], ALU.subtract,
               inc="dve")
            out_write(p)

    dbg = {}

    def dump(tag, ap, shape, ti, dt=BF16):
        if tag not in DEBUG.split(",") or ti != 0:
            return
        name = "dbg_" + tag
        t = nc.dram_tensor(name, list(shape), dt, kind="ExternalOutput").ap()
        dbg[name] = True
        em.fence(("gpsimd",), ("dve", "act", "pool", "pe"))
        em.dma("gpsimd", t, ap, "st")
        em.fence(("vector", "scalar", "gpsimd", "tensor", "sync"), ("st",))

    sidx = [0]

    def cut_out():
        em.fence(("gpsimd",), ("dve", "act", "pool", "pe"))
        em.dma("gpsimd", y_d[0, 0:128, 0:512], dm1[:], "st")


    def tile(ti):
        seq_start = (ti % 4) == 0
        b = ti // 4
        t0 = (ti % 4) * NT

        em.fence(("vector", "gpsimd", "scalar", "tensor", "sync"),
                 tuple(em.cnt.keys()))
        xle = xln[:].rearrange("p (c k) -> p c k", k=513)
        if seq_start:
            MEMSET("vector", xle[:, :, 0:1], 0.0, inc="dve")
        else:
            CP("vector", xle[:, :, 0:1], carry_s[:].unsqueeze(2), inc="dve")
        em.wait("gpsimd", "st")
        src = x_d[b, :, t0:t0 + NT].rearrange("(n p) c -> p n c", p=128)
        em.dma("gpsimd", xin[:].rearrange("p (n c) -> p n c", n=16), src,
               "xld1")
        em.fence(("scalar", "vector", "tensor"), ("xld1",))
        for p in range(16):
            ACT(pc(xb16, p), pc(xin, p), AF.Copy, inc="act")
        em.fence(("tensor", "vector"), ("act",))

        def ln1_write(p):
            TT("vector", xln[:, p * 513 + 1:p * 513 + 513], dmb1[:], pb[3][:],
               ALU.mult, inc="dve")
        ln_stats_apply(xb16, ln1_write)
        CP("vector", carry_s[:].unsqueeze(2), xle[:, :, 512:513], inc="dve")
        dump("xln", xln[:], [128, 16 * 513], ti)

        if CUT == "a1":
            return cut_out()
        em.fence(("vector",), ("dve",))
        TT("vector", delta[:].rearrange("p (c k) -> p c k", k=512),
           xle[:, :, 0:512], xle[:, :, 1:513], ALU.subtract, inc="dve")
        for p in range(16):
            STT("vector", pc(tmp_b, p), pc(delta, p), col(MUX + p),
                xln[:, p * 513 + 1:p * 513 + 513], ALU.mult, ALU.add,
                inc="dve")
        em.fence(("tensor",), ("act", "dve"))
        for m in range(2):
            nm = 128 if m == 0 else 32
            for ki in range(16):
                MM(pb[4 + m][0:nm, :],
                   wx1_s[:, ki * 160 + m * 128:ki * 160 + m * 128 + nm],
                   pc(tmp_b, ki), ki == 0, ki == 15,
                   inc="pe" if ki == 15 else None)
        em.wait("scalar", "pe")
        for m, (zt, nm) in enumerate(((zb0, 128), (zb1, 32))):
            em.wait("scalar", "dve")
            ACT(dm1[0:nm, :], pb[4 + m][0:nm, :], AF.Exp, scale=2.0, inc="act")
            em.wait("vector", "act")
            TS("vector", dm2[0:nm, :], dm1[0:nm, :], 1.0, ALU.add, inc="dve")
            em.op("vector", lambda e, nm=nm:
                  e.reciprocal(dm1[0:nm, :], dm2[0:nm, :]), inc="dve")
            TS("vector", zt[0:nm, :], dm1[0:nm, :], -2.0, ALU.mult, inc="dve",
               s2=1.0, op1=ALU.add)
        em.fence(("tensor",), ("dve",))

        mus_tt = {}
        mus_add = {}
        mix_done = {}

        def mus(i, dst_mix):
            em.fence(("tensor",), ("act", "dve"))
            if i < 4:
                lhs, base = wx2a_s, 32 * i
                zsl = zb0[32 * i:32 * i + 32, :]
            else:
                lhs, base = wx2b_s, 0
                zsl = zb1[0:32, :]
            for m in range(16):
                bank = m % 2
                if m - 2 in mus_tt:
                    em.wait("tensor", "dve", mus_tt[m - 2])
                MM(pb[bank][:], lhs[base:base + 32, m * 128:m * 128 + 128],
                   zsl, True, True, tp=(base, 0), inc="pe")
                em.wait("vector", "pe")
                if m in mus_add:
                    em.wait("vector", "dve", mus_add[m])
                TT("vector", pc(tmp_b, m), pb[bank][:], pc(delta, m), ALU.mult)
                mus_tt[m] = em.mark("dve")
                TT("vector", pc(dst_mix, m), pc(tmp_b, m),
                   xln[:, m * 513 + 1:m * 513 + 513], ALU.add)
                mus_add[m] = em.mark("dve")
            mix_done[i] = em.mark("dve")

        def proj(dscr, name, nm, mix, dst, mix_i=None, cap=99):
            nm = min(nm, cap)
            wait_ready("sync", name)
            if mix_i is not None:
                em.wait("tensor", "dve", mix_done[mix_i])
            evm = {}
            em.fence(("tensor",), ("act",))
            for m in range(nm):
                ch = f"wld{m % 2}"
                em.wait("sync", "pewc", max(0, em.mark("pewc") - 1))
                em.dma("sync", wcol[m % 2][:].rearrange("p (k c) -> p k c", k=16),
                       dscr[m].rearrange("k p c -> p k c"), ch)
                em.wait("tensor", ch)
                bank = 4 + (m % 2)
                if m - 2 in evm:
                    em.wait("tensor", "act", evm[m - 2])
                for ki in range(16):
                    MM(pb[bank][:], wcol[m % 2][:, ki * 128:ki * 128 + 128],
                       pc(mix, ki), ki == 0, ki == 15,
                       inc="pewc" if ki == 15 else None)
                em.wait("scalar", "pewc")
                if GDST and dst is sg_s:
                    ACT(dmb4[:], pb[bank][:], AF.Copy, inc="act")
                else:
                    ACT(pc(dst, m), pb[bank][:], AF.Copy, inc="act")
                evm[m] = em.mark("act")

        mus(0, xmix[0])
        proj(wr_sc, "wr", 8, xmix[0], r_raw, mix_i=0)
        mus(1, xmix[1])
        em.wait("tensor", "dve", mix_done[1])
        for ki in range(16):
            MM(pb[6][0:64, :], ww1_s[:, ki * 64:ki * 64 + 64], pc(xmix[1], ki),
               ki == 0, ki == 15, inc="pe" if ki == 15 else None)
        em.wait("scalar", "pe")
        em.wait("scalar", "dve")
        ACT(dm1[0:64, :], pb[6][0:64, :], AF.Exp, scale=2.0, inc="act")
        em.wait("vector", "act")
        TS("vector", dm2[0:64, :], dm1[0:64, :], 1.0, ALU.add, inc="dve")
        em.op("vector", lambda e: e.reciprocal(dm1[0:64, :], dm2[0:64, :]),
              inc="dve")
        TS("vector", tanha[0:64, :], dm1[0:64, :], -2.0, ALU.mult, inc="dve",
           s2=1.0, op1=ALU.add)
        MEMSET("vector", tanha[64:65, :], 1.0, inc="dve")
        em.fence(("tensor",), ("dve",))
        wem = {}
        em.fence(("tensor",), ("act",))
        for m in range(8):
            bank = 4 + (m % 2)
            if m - 2 in wem:
                em.wait("tensor", "act", wem[m - 2])
            MM(pb[bank][:], ww2_s[:, m * 128:m * 128 + 128], tanha[:],
               True, True, inc="pe")
            em.wait("scalar", "pe")
            ACT(pc(wexp, m), pb[bank][:], AF.Exp, inc="act")
            wem[m] = em.mark("act")
        mus(2, xmix[0])
        proj(wk_sc, "wk", 8, xmix[0], k_raw, mix_i=2)
        mus(3, xmix[1])
        wait_ready("sync", "wv")
        em.wait("tensor", "dve", mix_done[3])
        em.fence(("tensor",), ("dve", "act"))
        for vs in range(4):
            em.wait("tensor", "act")
            for ki in range(16):
                slot = ki % 2
                em.wait("sync", "pewc", max(0, em.mark("pewc") - 1))
                em.dma("sync", wcol[slot][:, 0:512], wv_sc[vs, ki],
                       f"wld{slot}")
                em.wait("tensor", f"wld{slot}")
                for tc in range(4):
                    MM(pb[tc][:],
                       xmix[1][:, ki * 512 + tc * 128:ki * 512 + tc * 128 + 128],
                       wcol[slot][:, 0:512], ki == 0, ki == 15,
                       inc="pewc" if tc == 3 else None)
            em.wait("scalar", "pewc")
            for tc in range(4):
                ACT(vtok[:, tc * 2048 + vs * 512:tc * 2048 + vs * 512 + 512],
                    pb[tc][:], AF.Copy, inc="act")
        mus(4, xmix[0])

        if CUT == "a2":
            return cut_out()
        zbc = col(ZRO).broadcast_to([128, 512])
        em.fence(("vector", "gpsimd", "scalar", "tensor"),
                 ("dve", "act", "pool", "pe"))

        for p in range(8):
            em.op("vector", lambda e, p=p: e.tensor_tensor_scan(
                pc(p512, p), zbc, pc(wexp, p), 0.0, ALU.add, ALU.add),
                inc="dve")
        if CUT == "a3scan":
            return cut_out()
        for p in range(8):
            CP("vector", psi_f[:, p * 18:p * 18 + 16],
               p512[:, p * 512 + 15:p * 512 + 512:32], inc="dve")
            CP("vector", psi_f[:, p * 18 + 16:p * 18 + 17],
               p512[:, p * 512 + 511:p * 512 + 512], inc="dve")
        em.fence(("scalar",), ("dve",))
        p3a = psi_f[:].rearrange("p (a c) -> p a c", a=8)
        TT("vector", dm1[:, 0:128].rearrange("p (a c) -> p a c", a=8),
           p3a[:, :, 0:16], p3a[:, :, 1:17], ALU.subtract, inc="dve")
        em.wait("scalar", "dve")
        for p in range(8):
            ACT(psi_b[:, p * 18:p * 18 + 16], dm1[:, p * 16:p * 16 + 16],
                AF.Exp, inc="act")
            ACT(psi_b[:, p * 18 + 17:p * 18 + 18],
                psi_f[:, p * 18:p * 18 + 1], AF.Exp, scale=-1.0, inc="act")
        if CUT == "a3psi":
            return cut_out()
        for p in range(8):
            mid = p512[:, p * 512 + 15:p * 512 + 512:32]
            midb = mid.unsqueeze(2).broadcast_to([128, 16, 32])
            p3 = pc(p512, p).rearrange("p (c j) -> p c j", c=16)
            em.wait("vector", "act" if p == 0 else "act")
            TT("vector", dm1[:].rearrange("p (c j) -> p c j", c=16), p3, midb,
               ALU.subtract, inc="dve")
            em.wait("scalar", "dve")
            ACT(pc(fk_s, p), dm1[:], AF.Exp, inc="act")
            em.wait("vector", "act")
            TT("vector", dm2[:], pc(wexp, p), dm1[:], ALU.subtract, inc="dve")
            em.wait("scalar", "dve")
            ACT(pc(fr_s, p), dm2[:], AF.Exp, inc="act")
        em.fence(("vector", "gpsimd"), ("act",))
        for p in range(8):
            TT("vector", pc(rt_s, p), pc(r_raw, p), pc(fr_s, p), ALU.mult,
               inc="dve")
            TT("vector", pc(kt_s, p), pc(k_raw, p), pc(fk_s, p), ALU.mult,
               inc="dve")
            TS("vector", pc(kb_s, p), pc(k_raw, p), col(BON + p), ALU.mult,
               inc="dve")
        em.fence(("vector",), ("dve",))
        for p in range(8):
            psv = psi_b[:, p * 18:p * 18 + 16].unsqueeze(2)
            psv = psv.broadcast_to([128, 16, 32])
            TT("vector", pc(khfm, p).rearrange("p (c j) -> p c j", c=16),
               pc(kt_s, p).rearrange("p (c j) -> p c j", c=16), psv, ALU.mult,
               inc="dve")
        if CUT == "a3fac":
            return cut_out()
        em.fence(("tensor",), ("dve",))
        tevm = {}
        for ii in range(32):
            kc, tc = ii // 4, ii % 4
            half = ii % 2
            if ii - 1 in tevm:
                em.wait("tensor", "dve", tevm[ii - 1])
            TRANS(tp_ps[:, half * 128:half * 128 + 128],
                  khfm[:, kc * 512 + tc * 128:kc * 512 + tc * 128 + 128],
                  inc="pe")
            em.wait("vector", "pe")
            CP("vector",
               kht[:, tc * 1024 + kc * 128:tc * 1024 + kc * 128 + 128],
               tp_ps[:, half * 128:half * 128 + 128], inc="dve")
            tevm[ii] = em.mark("dve")
        if CUT == "a3tp":
            return cut_out()
        si = sidx[0]
        if seq_start:
            MEMSET("vector", tstate[1 - si][:], 0.0, inc="dve")
        else:
            entv = psi_b[:, 17::18][:, 0:8]
            em.wait("vector", "act")
            TT("vector", tstate[1 - si][:].rearrange("p (a v) -> p a v", a=8),
               tstate[si][:].rearrange("p (a v) -> p a v", a=8),
               entv.unsqueeze(2).broadcast_to([128, 8, 64]), ALU.mult,
               inc="dve")
        sidx[0] = 1 - si
        if CUT == "a3ent":
            return cut_out()
        proj(wg_sc, "wg", 16, xmix[0], sg_s, mix_i=4, cap=GPN)

        dump("rraw", r_raw[:], [128, 8 * 512], ti)
        dump("kts", kt_s[:], [128, 8 * 512], ti)
        dump("rts", rt_s[:], [128, 8 * 512], ti)
        dump("kht", kht[:], [128, 4 * 1024], ti)
        dump("vtok", vtok[:], [128, 4 * 2048], ti)
        dump("sgs", sg_s[:], [128, 16 * 512], ti)
        dump("p512", p512[:], [128, 8 * 512], ti, dt=F32)
        dump("psib", psi_b[:], [128, 8 * 18], ti)
        if CUT == "a3":
            return cut_out()
        em.fence(("tensor", "vector", "gpsimd", "scalar"),
                 ("dve", "pool", "act", "pe"))
        msu = cbf_s[:, MSU:MSU + 32].unsqueeze(1).broadcast_to([128, 16, 32])
        msd = cbf_s[:, MSD:MSD + 32].unsqueeze(1).broadcast_to([128, 16, 32])
        cur = sidx[0]
        oev_dve = {}
        for blk in range(min(NBLK, max(SBLK, 1))):
            em.fence(("tensor",), ("dve", "pool", "act"))
            for sg in range(4):
                if sg > 0:
                    em.wait("tensor", "pe")
                strip = 32 * sg
                for h in range(sg, NH, 4):
                    pchk = h // 4
                    for ci in range(4):
                        c = blk * 4 + ci
                        co = pchk * 512 + c * 32
                        lk = kt_s[strip:strip + 32, co:co + 32]
                        lb = kb_s[strip:strip + 32, co:co + 32]
                        rr = rt_s[strip:strip + 32, co:co + 32]
                        rw = r_raw[strip:strip + 32, co:co + 32]
                        colo = 32 * (h % 16)
                        MM(pb[h // 16][32 * ci:32 * ci + 32, colo:colo + 32],
                           lk, rr, True, True, tp=(strip, 32 * ci))
                        MM(pb[2 + h // 16][32 * ci:32 * ci + 32,
                                           colo:colo + 32],
                           lb, rw, True, True, tp=(strip, 32 * ci),
                           inc="pe" if (h >= NH - 4 and ci == 3) else None)
            em.fence(("vector",), ("pe",))
            for half in range(2):
                TT("vector", dmb1[:].rearrange("p (a j) -> p a j", a=16),
                   pb[half][:].rearrange("p (a j) -> p a j", a=16), msu,
                   ALU.mult, inc="dve")
                TT("vector", dmb2[:].rearrange("p (a j) -> p a j", a=16),
                   pb[2 + half][:].rearrange("p (a j) -> p a j", a=16), msd,
                   ALU.mult, inc="dve")
                TT("vector", st_s[:, half * 512:half * 512 + 512], dmb1[:],
                   dmb2[:], ALU.add, inc="dve")
            if blk == 0:
                dump("sts0", st_s[:], [128, 1024], ti)
            if SPART == "sc":
                continue
            em.fence(("tensor",), ("dve",))
            for ci in range(4):
                c = blk * 4 + ci
                tc, tst = c // 4, 32 * (c % 4)
                sprev = tstate[cur]
                ibank = pb[ci]
                ebank = pb[4 + (ci % 2)]
                if ci >= 2:
                    em.wait("tensor", "dve", oev_dve.get(ci - 2, 0))
                for h in range(NH):
                    colo = (h // 2) * 32
                    lV = vtok[tst:tst + 32,
                              tc * 2048 + h * 64:tc * 2048 + h * 64 + 64]
                    rS = st_s[tst:tst + 32,
                              (h // 16) * 512 + 32 * (h % 16):
                              (h // 16) * 512 + 32 * (h % 16) + 32]
                    MM(ibank[64 * (h % 2):64 * (h % 2) + 64, colo:colo + 32],
                       lV, rS, True, True, tp=(tst, 64 * (h % 2)),
                       inc="pe" if h == NH - 1 else None)
                if SPART == "oi":
                    continue
                for sg in range(4):
                    em.wait("tensor", "pe")
                    strip = 32 * sg
                    for h in range(sg, NH, 4):
                        colo = (h // 2) * 32
                        lT = sprev[strip:strip + 32,
                                   (h // 4) * 64:(h // 4) * 64 + 64]
                        rr = rt_s[strip:strip + 32,
                                  (h // 4) * 512 + c * 32:
                                  (h // 4) * 512 + c * 32 + 32]
                        MM(ebank[64 * (h % 2):64 * (h % 2) + 64,
                                 colo:colo + 32],
                           lT, rr, True, True, tp=(strip, 64 * (h % 2)),
                           inc="pe" if h >= NH - 4 else None)
                em.wait("tensor", "pe")
                for h in range(NH):
                    strip = 32 * (h % 4)
                    lkh = kht[tst:tst + 32,
                              tc * 1024 + h * 32:tc * 1024 + h * 32 + 32]
                    rV = vtok[tst:tst + 32,
                              tc * 2048 + h * 64:tc * 2048 + h * 64 + 64]
                    MM(pb[6][strip:strip + 32,
                             (h // 4) * 64:(h // 4) * 64 + 64],
                       lkh, rV, True, True, tp=(tst, strip),
                       inc="pe" if h == NH - 1 else None)
                if SPART == "os":
                    continue
                em.fence(("vector", "scalar"), ("pe",))
                psv = psi_b[:, c::18][:, 0:8].unsqueeze(2)
                TT("vector", dmb3[:].rearrange("p (a v) -> p a v", a=8),
                   sprev[:].rearrange("p (a v) -> p a v", a=8),
                   psv.broadcast_to([128, 8, 64]), ALU.mult, inc="dve")
                TT("vector", tstate[1 - cur][:], dmb3[:], pb[6][:], ALU.add,
                   inc="dve")
                cur = 1 - cur
                em.fence(("tensor",), ("dve",))
                sq = dmb4 if ci % 2 == 0 else dmb5
                ACT(sq[:], ibank[:], AF.Copy, inc="act")
                em.wait("vector", "act")
                ob3c = ob16[:].rearrange("p (v c) -> p v c", v=16)
                TT("vector",
                   ob3c[:, :, blk * 128 + 32 * ci:blk * 128 + 32 * ci + 32],
                   sq[:].rearrange("p (v c) -> p v c", v=16), ebank[:]
                   .rearrange("p (v c) -> p v c", v=16), ALU.add, inc="dve")
                oev_dve[ci] = em.mark("dve")
        if SBLK < 99 or SPART:
            return cut_out()
        dump("ob16", ob16[:], [128, 16 * 512], ti)
        em.fence(("tensor",), ("act", "dve", "pool"))
        for vc in range(16):
            ep = cbf_s[:, EPA + 32 * vc:EPA + 32 * vc + 32]
            MM(pb[0][0:32, :], ep, pc(ob16, vc), vc == 0, vc == 15,
               inc="pe" if vc == 15 else None)
        em.wait("scalar", "pe")
        ACT(gnm_s[:], pb[0][0:32, :], AF.Copy, scale=1.0 / HV, inc="act")
        em.fence(("tensor",), ("act",))
        ocm = {}
        for vc in range(16):
            em.wait("tensor", "dve", max(0, em.mark("dve") - 3))
            MM(pb[2 + vc % 4][:], gpat_s[0:32, vc * 128:vc * 128 + 128],
               gnm_s[:], True, True, inc="pe")
            em.wait("vector", "pe")
            TT("vector", pc(oc_b, vc), pc(ob16, vc), pb[2 + vc % 4][:],
               ALU.subtract, inc="dve")
            ocm[vc] = em.mark("dve")
        for vc in range(16):
            sq = dmb3 if vc % 2 == 0 else dmb2
            em.wait("scalar", "dve", ocm[vc])
            em.wait("scalar", "pe", max(0, em.mark("pe") - 1))
            ACT(sq[:], pc(oc_b, vc), AF.Square, inc="act")
            em.wait("tensor", "act")
            MM(pb[1][0:32, :], cbf_s[:, EPA + 32 * vc:EPA + 32 * vc + 32],
               sq[:], vc == 0, vc == 15, inc="pe")
        em.wait("scalar", "pe")
        ACT(gnscr[:], pb[1][0:32, :], AF.Ln, bias=cols_s[0:32, EPSC:EPSC + 1],
            scale=1.0 / HV, inc="act")
        ACT(gnr_s[:], gnscr[:], AF.Exp, scale=-0.5, inc="act")
        em.fence(("tensor",), ("act",))
        for vc in range(16):
            sil = dmb4 if vc % 2 == 0 else dmb5
            mid = dmb1 if vc % 2 == 0 else dmb3
            em.wait("tensor", "dve", max(0, em.mark("dve") - 3))
            MM(pb[2 + vc % 4][:], gpat_s[0:32, vc * 128:vc * 128 + 128],
               gnr_s[:], True, True, inc="pe")
            em.wait("scalar", "dve", max(0, em.mark("dve") - 1))
            ACT(sil[:], pc(sg_s, vc), AF.Silu, inc="act")
            em.wait("vector", "pe")
            em.wait("vector", "act")
            STT("vector", mid[:], pc(oc_b, vc), col(GNW), pb[2 + vc % 4][:],
                ALU.mult, ALU.mult, inc="dve")
            STT("vector", pc(oprime, vc), mid[:], col(GNB), sil[:],
                ALU.add, ALU.mult, inc="dve")
        dump("oprime", oprime[:], [128, 16 * 512], ti)

        if CUT == "a4":
            return cut_out()
        em.fence(("vector", "gpsimd", "scalar", "tensor"),
                 ("dve", "act", "pool", "pe"))
        wait_ready("sync", "wo")
        em.fence(("tensor", "sync"), ("dve",))
        src = x_d[b, :, t0:t0 + NT].rearrange("(n p) c -> p n c", p=128)
        em.dma("gpsimd", xrel[:].rearrange("p (n c) -> p n c", n=16), src,
               "xld2")
        evm5 = {}
        em.fence(("tensor",), ("act", "dve"))
        for m in range(16):
            ch = f"wld{m % 2}"
            em.wait("sync", "pewc", max(0, em.mark("pewc") - 1))
            em.dma("sync", wcol[m % 2][:].rearrange("p (k c) -> p k c", k=16),
                   wo_sc[m].rearrange("k p c -> p k c"), ch)
            em.wait("tensor", ch)
            bank = 4 + (m % 2)
            if m - 2 in evm5:
                em.wait("tensor", "act", evm5[m - 2])
            for ki in range(16):
                MM(pb[bank][:], wcol[m % 2][:, ki * 128:ki * 128 + 128],
                   pc(oprime, ki), ki == 0, ki == 15,
                   inc="pewc" if ki == 15 else None)
            em.wait("scalar", "pewc")
            ACT(pc(ao_b, m), pb[bank][:], AF.Copy, inc="act")
            evm5[m] = em.mark("act")
        em.fence(("tensor", "vector", "scalar"), ("act",))
        em.fence(("vector", "gpsimd"), ("xld2",))

        def ln2_write(p):
            TT("vector", dmb2[:], dmb1[:], pb[3][:], ALU.mult, inc="dve")
            TT("vector", pc(xprime, p), pc(xrel, p), dmb2[:], ALU.add,
               inc="dve")
        ln_stats_apply(ao_b, ln2_write)
        dump("xprime", xprime[:], [128, 16 * 512], ti, dt=F32)

        if CUT == "a5":
            return cut_out()
        em.fence(("vector", "gpsimd", "scalar", "tensor"),
                 ("dve", "act", "pool", "pe"))

        for p in range(16):
            ACT(pc(xln3, p), pc(xprime, p), AF.Copy, inc="act")
        em.fence(("tensor", "vector"), ("act",))

        def ln3_write(p):
            TT("vector", pc(xln3, p), dmb1[:], pb[3][:], ALU.mult, inc="dve")
        ln_stats_apply(xln3, ln3_write)
        wait_ready("sync", "gt")
        wait_ready("sync", "up")
        em.fence(("tensor", "sync"), ("dve",))
        gum = {}
        gam = {}
        em.fence(("tensor",), ("act", "dve"))
        for m in range(44):
            ch = f"wld{m % 2}"
            if m - 2 in gum:
                em.wait("sync", "dve", gum[m - 2])
            em.dma("sync", gcol[m % 2][:].rearrange("p (k c) -> p k c", k=16),
                   gt_sc[m].rearrange("k p c -> p k c"), ch)
            em.dma("sync", ucol[m % 2][:].rearrange("p (k c) -> p k c", k=16),
                   up_sc[m].rearrange("k p c -> p k c"), ch)
            em.wait("tensor", ch)
            gb = pb[4] if m % 2 == 0 else pb[6]
            ub = pb[5] if m % 2 == 0 else pb[0]
            if m - 2 in gum:
                em.wait("tensor", "dve", gum[m - 2])
            if m - 2 in gam:
                em.wait("tensor", "act", gam[m - 2])
            for ki in range(16):
                MM(gb[:], gcol[m % 2][:, ki * 128:ki * 128 + 128],
                   pc(xln3, ki), ki == 0, ki == 15)
                MM(ub[:], ucol[m % 2][:, ki * 128:ki * 128 + 128],
                   pc(xln3, ki), ki == 0, ki == 15,
                   inc="pe" if ki == 15 else None)
            em.wait("scalar", "pe")
            sil = dmb1 if m % 2 == 0 else dmb2
            if m - 2 in gum:
                em.wait("scalar", "dve", gum[m - 2])
            ACT(sil[:], gb[:], AF.Silu, scale=col(SGA + m), inc="act")
            gam[m] = em.mark("act")
            em.wait("vector", "act")
            em.op("vector", lambda e, m=m, ub=ub, sil=sil: e.scalar_tensor_tensor(
                pc(gu_s, m), ub[:], col(SUP + m), sil[:], ALU.mult, ALU.mult))
            gum[m] = em.mark("dve")

        dump("xln3", xln3[:], [128, 16 * 512], ti)
        dump("gus", gu_s[:], [128, 44 * 512], ti)
        if CUT == "f1":
            return cut_out()
        em.fence(("vector", "gpsimd", "scalar", "tensor"),
                 ("dve", "act", "pool", "pe"))
        wait_ready("sync", "dn")
        em.fence(("tensor", "sync"), ("dve",))
        evm6 = {}
        em.fence(("tensor",), ("act",))
        for m in range(16):
            ch = f"wld{m % 2}"
            em.wait("sync", "pewc", max(0, em.mark("pewc") - 1))
            em.dma("sync", dcol[m % 2][:].rearrange("p (k c) -> p k c", k=44),
                   dn_sc[m].rearrange("k p c -> p k c"), ch)
            em.wait("tensor", ch)
            bank = 4 + (m % 2)
            if m - 2 in evm6:
                em.wait("tensor", "act", evm6[m - 2])
            for ki in range(44):
                MM(pb[bank][:], dcol[m % 2][:, ki * 128:ki * 128 + 128],
                   pc(gu_s, ki), ki == 0, ki == 43,
                   inc="pewc" if ki == 43 else None)
            em.wait("scalar", "pewc")
            ACT(pc(dnb, m), pb[bank][:], AF.Copy, scale=col(SDN + m),
                inc="act")
            evm6[m] = em.mark("act")
        em.fence(("tensor", "vector", "scalar"), ("act",))

        dump("dnb", dnb[:], [128, 16 * 512], ti)

        def ln4_write(p):
            TT("vector", dmb2[:], dmb1[:], pb[3][:], ALU.mult, inc="dve")
            TT("vector", pc(y_s, p), pc(xprime, p), dmb2[:], ALU.add,
               inc="dve")
        ln_stats_apply(dnb, ln4_write)
        em.fence(("gpsimd",), ("dve",))
        dst = y_d[b, :, t0:t0 + NT].rearrange("(n p) c -> p n c", p=128)
        em.dma("gpsimd", dst, y_s[:].rearrange("p (n c) -> p n c", n=16), "st")

    if CUT == "pre":
        em.fence(("gpsimd",), ("dve", "act", "pool", "pe", "pldc"))
        for chn, v0, ch1, v1 in [pre_ready[k] for k in pre_ready]:
            em.wait("gpsimd", chn, v0)
            em.wait("gpsimd", ch1, v1)
        em.dma("gpsimd", y_d[0, 0:128, 0:512], dm1[:], "st")
    else:
        for ti in range(n_tiles):
            tile(ti)

    em.wait("gpsimd", "st")
    em.wait("sync", "st")
    em.emit()
    return nc


def _host_inputs(inputs):
    x = np.ascontiguousarray(inputs["L0_attn_hidden_states"])
    mu_x = np.asarray(inputs["L0_attn_mu_x"], np.float32)
    Wx1 = np.asarray(inputs["L0_attn_Wx1"], np.float32)
    Wx2 = np.asarray(inputs["L0_attn_Wx2"], np.float32)
    Wr = np.ascontiguousarray(inputs["L0_attn_Wr"], np.float32)
    Ww1 = np.asarray(inputs["L0_attn_Ww1"], np.float32)
    Ww2 = np.asarray(inputs["L0_attn_Ww2"], np.float32)
    bw = np.asarray(inputs["L0_attn_bw"], np.float32)
    Wk = np.ascontiguousarray(inputs["L0_attn_Wk"], np.float32)
    Wv = np.ascontiguousarray(inputs["L0_attn_Wv"], np.float32)
    Wg = np.ascontiguousarray(inputs["L0_attn_Wg"], np.float32)
    bonus = np.asarray(inputs["L0_attn_bonus"], np.float32)
    gn_w = np.asarray(inputs["L0_attn_gn_w"], np.float32)
    gn_b = np.asarray(inputs["L0_attn_gn_b"], np.float32)
    Wo = np.ascontiguousarray(inputs["L0_attn_Wo"], np.float32)
    wqg = np.asarray(inputs["L0_ffn_wq_gate"], np.int32)
    sgate = np.asarray(inputs["L0_ffn_s_gate"], np.float32)
    wqu = np.asarray(inputs["L0_ffn_wq_up"], np.int32)
    sup = np.asarray(inputs["L0_ffn_s_up"], np.float32)
    wqd = np.asarray(inputs["L0_ffn_wq_down"], np.int32)
    sdown = np.asarray(inputs["L0_ffn_s_down"], np.float32)

    wx1_l = np.ascontiguousarray(
        Wx1.reshape(16, 128, 160).transpose(1, 0, 2).reshape(128, 16 * 160))
    wx2a_l = np.ascontiguousarray(Wx2[0:4].reshape(128, D))
    wx2b_l = np.ascontiguousarray(Wx2[4])
    ww1_l = np.ascontiguousarray(
        Ww1.reshape(16, 128, 64).transpose(1, 0, 2).reshape(128, 16 * 64))
    ww2_l = np.ascontiguousarray(np.concatenate([Ww2, bw[None, :]], axis=0))
    wqg_t = np.ascontiguousarray(wqg.T)
    wqu_t = np.ascontiguousarray(wqu.T)
    wqd_t = np.ascontiguousarray(wqd.T)

    cols = np.zeros((128, NCOLS), np.float32)
    cols[:, BON:BON + 8] = bonus.reshape(-1).reshape(8, 128).T
    cols[:, MUX:MUX + 16] = mu_x.reshape(16, 128).T
    cols[:, GNW] = np.tile(gn_w, 2)
    cols[:, GNB] = np.tile(gn_b, 2)
    cols[:, SGA:SGA + 44] = sgate.reshape(44, 128).T
    cols[:, SUP:SUP + 44] = sup.reshape(44, 128).T
    cols[:, SDN:SDN + 16] = sdown.reshape(16, 128).T
    cols[:, EPSC] = EPS

    cbf = np.zeros((128, NBF), np.float32)
    j = np.arange(32)
    m32 = (j[:, None] < j[None, :]).astype(np.float32)
    d32 = np.eye(32, dtype=np.float32)
    cbf[:, MSU:MSU + 32] = np.tile(m32, (4, 1))
    cbf[:, MSD:MSD + 32] = np.tile(d32, (4, 1))
    cbf[:, IDE:IDE + 128] = np.eye(128, dtype=np.float32)
    for vc in range(16):
        ep = np.zeros((128, 32), np.float32)
        ep[0:64, 2 * vc] = 1.0
        ep[64:128, 2 * vc + 1] = 1.0
        cbf[:, EPA + 32 * vc:EPA + 32 * vc + 32] = ep
    cbf[:, ONE] = 1.0
    cbf[:, ONE + 1] = 1.0
    cbf_b = cbf.astype(bf16)

    gpat = np.zeros((32, 2048), np.float32)
    for pcn in range(16):
        gp = np.zeros((32, 128), np.float32)
        gp[2 * pcn, 0:64] = 1.0
        gp[2 * pcn + 1, 64:128] = 1.0
        gpat[:, 128 * pcn:128 * pcn + 128] = gp
    gpat_b = gpat.astype(bf16)

    shared = dict(wx1=wx1_l, wx2a=wx2a_l, wx2b=wx2b_l, ww1=ww1_l,
                  ww2aug=ww2_l, wr=Wr, wk=Wk, wv=Wv, wg=Wg, wo=Wo,
                  wqg=wqg_t, wqu=wqu_t, wqd=wqd_t, cols=cols, cbf=cbf_b,
                  gpat=gpat_b)
    maps = []
    for c in range(8):
        xt = np.ascontiguousarray(x[2 * c:2 * c + 2].transpose(0, 2, 1))
        m = dict(shared)
        m["x"] = xt
        maps.append(m)
    return maps


def kernel(**inputs):
    nc = build(N_TILES)
    in_maps = _host_inputs(inputs)
    res = run_bass_kernel_spmd(nc, in_maps, core_ids=list(range(8)))
    y = np.empty((16, LSEQ, D), np.float32)
    for c in range(8):
        yc = np.asarray(res.results[c]["y"])
        y[2 * c:2 * c + 2] = yc.transpose(0, 2, 1)
    return y


# version 03 (final)
# speedup vs baseline: 1.2745x; 1.1138x over previous
import os
import numpy as np
import ml_dtypes
import concourse.bass as bass
import concourse.mybir as mybir
from concourse.bass_utils import run_bass_kernel_spmd

F32 = mybir.dt.float32
BF16 = mybir.dt.bfloat16
I32 = mybir.dt.int32
AF = mybir.ActivationFunctionType
ALU = mybir.AluOpType
bf16 = ml_dtypes.bfloat16

D = 2048
LSEQ = 2048
NT = 512
NH, HQK, HV = 32, 32, 64
DK, DV, FF = 1024, 2048, 5632
CH = 32
NCH = NT // CH
NBLK = NT // 128
EPS = 1e-5

PRE, A1, A2, A3, A4, A5, F1, F2 = range(8)

N_TILES = int(os.environ.get("MK_NTILES", "0")) or 8
DEBUG = os.environ.get("MK_DEBUG", "")
CUT = os.environ.get("MK_CUT", "")
TPV = os.environ.get("MK_TPV", "")
TPN = int(os.environ.get("MK_TPN", "32"))
GPN = int(os.environ.get("MK_GPN", "16"))
GDST = os.environ.get("MK_GDST", "")
SBLK = int(os.environ.get("MK_SBLK", "99"))
SPART = os.environ.get("MK_SPART", "")

BON = 0
MUX = 8
GNW = 24
GNB = 25
SGA = 26
SUP = 70
SDN = 114
EPSC = 130
ZRO = 131
NCOLS = 132
MSU = 0
MSD = 32
IDE = 64
EPA = 192
ONE = 704
NBF = 706


class Em:
    ENGINES = ("sync", "gpsimd", "vector", "scalar", "tensor")

    def __init__(self, nc):
        self.nc = nc
        self.items = {e: [] for e in self.ENGINES}
        self.sems = {}
        self.cnt = {}
        self.last_wait = {e: {} for e in self.ENGINES}

    def _sem(self, name):
        if name not in self.sems:
            self.sems[name] = self.nc.alloc_semaphore(name)
            self.cnt[name] = 0
        return self.sems[name]

    def mark(self, ch):
        self._sem(ch)
        return self.cnt[ch]

    def wait(self, eng, ch, val=None):
        self._sem(ch)
        v = self.cnt[ch] if val is None else val
        if v <= 0 or self.last_wait[eng].get(ch, -1) >= v:
            return
        self.last_wait[eng][ch] = v
        self.items[eng].append(("w", ch, v))

    ECH = {"vector": "dve", "scalar": "act", "gpsimd": "pool"}

    def op(self, eng, fn, inc=None, inc_n=1, dma=False):
        ch = self.ECH.get(eng)
        if ch is not None and not dma:
            assert inc is None or inc == ch, (eng, inc)
            self.wait(eng, ch)
            inc, inc_n = ch, 1
        if inc:
            self._sem(inc)
            self.cnt[inc] += inc_n
        self.items[eng].append(("o", fn, inc, inc_n))

    def dma(self, eng, out, in_, inc):
        self.op(eng, lambda e, o=out, i=in_: e.dma_start(out=o, in_=i), inc, 16,
                dma=True)

    def fence(self, consumers, producers):
        for eng in consumers:
            for ch in producers:
                self.wait(eng, ch)

    def emit(self):
        with self.nc.Block() as block:
            for eng in self.ENGINES:
                items = self.items[eng]
                if not items:
                    continue

                def mk(items=items):
                    def body(e):
                        for it in items:
                            if it[0] == "w":
                                e.wait_ge(self.sems[it[1]], it[2])
                            else:
                                ins = it[1](e)
                                if it[2] is not None:
                                    ins.then_inc(self.sems[it[2]], it[3])
                    return body
                getattr(block, eng)(mk())


class Plan:
    def __init__(self, nc, base, size):
        self.nc = nc
        self.base = base
        self.size = size
        self.items = []

    def tensor(self, name, shape, dtype, s, e, align=128):
        isz = 2 if dtype == BF16 else 4
        nb = int(np.prod(shape[1:])) * isz if len(shape) > 1 else isz
        nb = (nb + align - 1) // align * align
        spans = sorted((it[1], it[1] + it[2]) for it in self.items
                       if not (e < it[3] or s > it[4]))
        off = 0
        for a, b in spans:
            if off + nb <= a:
                break
            off = max(off, b)
        assert off + nb <= self.size, \
            f"sbuf overflow: {name} needs {nb} at {off}, cap {self.size}"
        self.items.append((name, off, nb, s, e))
        return self.nc.alloc_sbuf_tensor_at(name, list(shape), dtype,
                                            offset=self.base + off)


def build(n_tiles=8):
    nc = bass.Bass("TRN2", target_bir_lowering=False)

    def din(name, shape, dt=F32):
        return nc.dram_tensor(name, list(shape), dt, kind="ExternalInput").ap()

    x_d = din("x", [2, D, LSEQ])
    wx1_d = din("wx1", [128, 16 * 160])
    wx2a_d = din("wx2a", [128, D])
    wx2b_d = din("wx2b", [32, D])
    ww1_d = din("ww1", [128, 16 * 64])
    ww2_d = din("ww2aug", [65, DK])
    wr_d = din("wr", [D, DK])
    wk_d = din("wk", [D, DK])
    wv_d = din("wv", [D, DV])
    wg_d = din("wg", [D, DV])
    wo_d = din("wo", [DV, D])
    wqg_d = din("wqg", [D, FF], I32)
    wqu_d = din("wqu", [D, FF], I32)
    wqd_d = din("wqd", [FF, D], I32)
    cols_d = din("cols", [128, NCOLS])
    cbf_d = nc.dram_tensor("cbf", [128, NBF], BF16, kind="ExternalInput").ap()
    gpat_d = nc.dram_tensor("gpat", [32, 2048], BF16, kind="ExternalInput").ap()

    y_d = nc.dram_tensor("y", [2, D, LSEQ], F32, kind="ExternalOutput").ap()

    def scr(name, shape):
        return nc.dram_tensor(name, list(shape), BF16, kind="Internal").ap()

    wr_sc = scr("wr_sc", [8, 16, 128, 128])
    wk_sc = scr("wk_sc", [8, 16, 128, 128])
    wv_sc = scr("wv_sc", [4, 16, 128, 512])
    wg_sc = scr("wg_sc", [16, 16, 128, 128])
    wo_sc = scr("wo_sc", [16, 16, 128, 128])
    gt_sc = scr("gt_sc", [44, 16, 128, 128])
    up_sc = scr("up_sc", [44, 16, 128, 128])
    dn_sc = scr("dn_sc", [16, 44, 128, 128])

    cap = nc.sbuf_bytes_remaining
    lo, hi = nc.bump_sbuf(cap - 256)
    pl = Plan(nc, lo, hi - lo)
    P0, P1 = PRE, F2

    wx1_s = pl.tensor("wx1_s", [128, 16 * 160], BF16, P0, P1)
    wx2a_s = pl.tensor("wx2a_s", [128, D], BF16, P0, P1)
    wx2b_s = pl.tensor("wx2b_s", [32, D], BF16, P0, P1)
    ww1_s = pl.tensor("ww1_s", [128, 16 * 64], BF16, P0, P1)
    ww2_s = pl.tensor("ww2_s", [65, DK], BF16, P0, P1)
    cols_s = pl.tensor("cols_s", [128, NCOLS], F32, P0, P1)
    cbf_s = pl.tensor("cbf_s", [128, NBF], BF16, P0, P1)
    gpat_s = pl.tensor("gpat_s", [32, 2048], BF16, P0, P1)
    tstate = [pl.tensor(f"tstate{i}", [128, 512], BF16, P0, P1)
              for i in range(2)]
    psi_f = pl.tensor("psi_f", [128, 8 * 18], F32, A3, A4)
    psi_b = pl.tensor("psi_b", [128, 8 * 18], BF16, A3, A4)
    zb0 = pl.tensor("zb0", [128, NT], BF16, A2, A2)
    zb1 = pl.tensor("zb1", [32, NT], BF16, A2, A2)
    tanha = pl.tensor("tanha", [65, NT], BF16, A2, A2)
    rows_b = pl.tensor("rows_b", [1, 1024], BF16, P0, P1)
    rows_f = pl.tensor("rows_f", [1, 1024], F32, P0, P1)
    gnm_s = pl.tensor("gnm_s", [32, 512], BF16, A4, A4)
    gnr_s = pl.tensor("gnr_s", [32, 512], BF16, A4, A4)
    gnscr = pl.tensor("gnscr", [32, 512], F32, A4, A4)
    dm1 = pl.tensor("dm1", [128, 512], F32, P0, P1)
    dm2 = pl.tensor("dm2", [128, 512], F32, P0, P1)
    dmb1 = pl.tensor("dmb1", [128, 512], BF16, P0, P1)
    dmb2 = pl.tensor("dmb2", [128, 512], BF16, P0, P1)
    dmb3 = pl.tensor("dmb3", [128, 512], BF16, P0, P1)
    dmb4 = pl.tensor("dmb4", [128, 512], BF16, P0, P1)
    dmb5 = pl.tensor("dmb5", [128, 512], BF16, P0, P1)
    carry_s = pl.tensor("carry_s", [128, 16], BF16, P0, P1)

    xin = pl.tensor("xin", [128, 16 * 512], F32, A1, A1)
    xb16 = pl.tensor("xb16", [128, 16 * 512], BF16, A1, A1)
    xln = pl.tensor("xln", [128, 16 * 513], BF16, A1, A2)
    delta = pl.tensor("delta", [128, 16 * 512], BF16, A2, A2)
    tmp_b = pl.tensor("tmp_b", [128, 16 * 512], BF16, A2, A2)
    xmix = [pl.tensor(f"xmix{i}", [128, 16 * 512], BF16, A2, A3 if i == 0 else A2)
            for i in range(2)]
    vtok = pl.tensor("vtok", [128, 4 * 2048], BF16, A2, A4)
    sg_s = pl.tensor("sg_s", [128, 16 * 512], BF16, A3, A4)
    r_raw = pl.tensor("r_raw", [128, 8 * 512], BF16, A2, A4)
    k_raw = pl.tensor("k_raw", [128, 8 * 512], BF16, A2, A3)
    wexp = pl.tensor("wexp", [128, 8 * 512], F32, A2, A3)
    p512 = pl.tensor("p512", [128, 8 * 512], F32, A3, A3)
    fk_s = pl.tensor("fk_s", [128, 8 * 512], BF16, A3, A3)
    fr_s = pl.tensor("fr_s", [128, 8 * 512], BF16, A3, A3)
    khfm = pl.tensor("khfm", [128, 8 * 512], BF16, A3, A3)
    rt_s = pl.tensor("rt_s", [128, 8 * 512], BF16, A3, A4)
    kt_s = pl.tensor("kt_s", [128, 8 * 512], BF16, A3, A4)
    kht = pl.tensor("kht", [128, 4 * 1024], BF16, A3, A4)
    kb_s = pl.tensor("kb_s", [128, 8 * 512], BF16, A3, A4)
    st_s = pl.tensor("st_s", [128, 1024], BF16, A4, A4)
    ob16 = pl.tensor("ob16", [128, 16 * 512], BF16, A4, A4)
    oc_b = pl.tensor("oc_b", [128, 16 * 512], BF16, A4, A4)
    oprime = pl.tensor("oprime", [128, 16 * 512], BF16, A4, A5)
    ao_b = pl.tensor("ao_b", [128, 16 * 512], BF16, A5, A5)
    xrel = pl.tensor("xrel", [128, 16 * 512], F32, A5, A5)
    xprime = pl.tensor("xprime", [128, 16 * 512], F32, A5, F2)
    xln3 = pl.tensor("xln3", [128, 16 * 512], BF16, F1, F1)
    gu_s = pl.tensor("gu_s", [128, 44 * 512], BF16, F1, F2)
    dnb = pl.tensor("dnb", [128, 16 * 512], BF16, F2, F2)
    y_s = pl.tensor("y_s", [128, 16 * 512], F32, F2, F2)
    wcol = [pl.tensor(f"wcol{i}", [128, 2048], BF16, A2, A5) for i in range(2)]
    gcol = [pl.tensor(f"gcol{i}", [128, 2048], BF16, F1, F1) for i in range(3)]
    ucol = [pl.tensor(f"ucol{i}", [128, 2048], BF16, F1, F1) for i in range(3)]
    dcol = [pl.tensor(f"dcol{i}", [128, 44 * 128], BF16, F2, F2)
            for i in range(2)]
    pst_i = [pl.tensor(f"pst_i{i}", [128, FF], I32, PRE, PRE) for i in range(2)]
    pst_b = [pl.tensor(f"pst_b{i}", [128, FF], BF16, PRE, PRE)
             for i in range(2)]
    pst_f = [pl.tensor(f"pst_f{i}", [128, 2816], F32, PRE, PRE)
             for i in range(2)]
    pst_o = [pl.tensor(f"pst_o{i}", [128, 2816], BF16, PRE, PRE)
             for i in range(2)]

    tp_ps = nc.alloc_psum_tensor("tp_ps", [128, 1024], BF16)
    pb = [nc.alloc_psum_tensor(f"pb{i}", [128, 512], F32) for i in range(7)]

    em = Em(nc)

    def col(idx):
        return cols_s[:, idx:idx + 1]

    def TT(eng, out, a, b, op, inc=None):
        em.op(eng, lambda e, o=out, x=a, y=b, p=op: e.tensor_tensor(o, x, y, p),
              inc)

    def STT(eng, out, in0, scalar, in1, op0, op1, inc=None):
        em.op(eng, lambda e, o=out, i0=in0, s=scalar, i1=in1, p0=op0, p1=op1:
              e.scalar_tensor_tensor(o, i0, s, i1, p0, p1), inc)

    def TS(eng, out, in0, s1, op0, inc=None, s2=None, op1=None):
        if op1 is None:
            em.op(eng, lambda e, o=out, i=in0, s=s1, p=op0:
                  e.tensor_scalar(o, i, s, None, p), inc)
        else:
            em.op(eng, lambda e, o=out, i=in0, a=s1, b=s2, p0=op0, p1=op1:
                  e.tensor_scalar(o, i, a, b, p0, p1), inc)

    def ACT(out, in_, func, bias=0.0, scale=1.0, inc=None):
        em.op("scalar", lambda e, o=out, i=in_, f=func, b=bias, s=scale:
              e.activation(o, i, f, bias=b, scale=s), inc)

    def CP(eng, out, in_, inc=None):
        em.op(eng, lambda e, o=out, i=in_: e.tensor_copy(o, i), inc)

    def MM(out, lhsT, rhs, start, stop, tp=None, inc=None):
        em.op("tensor", lambda e, o=out, l=lhsT, r=rhs, s0=start, s1=stop, t=tp:
              e.matmul(o, l, r, start=s0, stop=s1, tile_position=t,
                       skip_group_check=True), inc)

    def TRANS(out, in_, inc=None):
        em.op("tensor", lambda e, o=out, i=in_:
              e.transpose(o, i, cbf_s[:, IDE:IDE + 128]), inc)

    def MEMSET(eng, ap, v, inc=None):
        em.op(eng, lambda e, a=ap, vv=v: e.memset(a, vv), inc)

    def pc(t, i, c0=0, c1=512, w=512):
        return t[:, i * w + c0: i * w + c1]

    small = [(wx1_d, wx1_s, 16 * 160, 128), (wx2a_d, wx2a_s, D, 128),
             (wx2b_d, wx2b_s, D, 32), (ww1_d, ww1_s, 16 * 64, 128),
             (ww2_d, ww2_s, DK, 65)]
    for i, (dsrc, sdst, ncols, nprt) in enumerate(small):
        stg = pst_f[i % 2]
        ch = f"pld{i % 2}"
        em.wait("sync", "dve", em.mark("dve"))
        em.dma("sync", stg[0:nprt, 0:ncols], dsrc, ch)
        em.wait("vector", ch)
        CP("vector", sdst[0:nprt, 0:ncols], stg[0:nprt, 0:ncols], inc="dve")
    em.dma("sync", cols_s[:], cols_d, "pldc")
    em.dma("sync", cbf_s[:], cbf_d, "pldc")
    em.dma("sync", gpat_s[:], gpat_d, "pldc")
    MEMSET("vector", tstate[0][:], 0.0, inc="dve")
    MEMSET("vector", tstate[1][:], 0.0, inc="dve")

    pre_ready = {}
    st_gate = {0: None, 1: None}

    def conv_attn(name, dsrc, dscr, nm, w=128):
        for ki in range(16):
            s = ki % 2
            stch = f"pst_{name}{s}"
            em.wait("sync", "dve", em.mark("dve") - (1 if ki >= 1 else 0))
            if ki == 0:
                em.wait("sync", "pool", em.mark("pool"))
            ncol = dsrc.shape[1]
            em.dma("sync", pst_f[s][:, 0:ncol],
                   dsrc[128 * ki:128 * ki + 128, :], f"pld{s}")
            em.wait("vector", f"pld{s}")
            if st_gate[s]:
                em.wait("vector", st_gate[s][0], st_gate[s][1])
            CP("vector", pst_o[s][:, 0:ncol], pst_f[s][:, 0:ncol], inc="dve")
            em.wait("scalar", "dve")
            nmm = ncol // w
            src = pst_o[s][:, 0:ncol].rearrange("p (m c) -> p m c", m=nmm)
            dst = dscr[:, ki, :, :].transpose([1, 0, 2])
            em.dma("scalar", dst, src, stch)
            st_gate[s] = (stch, em.mark(stch))
        pre_ready[name] = (f"pst_{name}0", em.mark(f"pst_{name}0"),
                           f"pst_{name}1", em.mark(f"pst_{name}1"))

    conv_attn("wr", wr_d, wr_sc, 8)
    conv_attn("wk", wk_d, wk_sc, 8)
    conv_attn("wv", wv_d, wv_sc, 4, w=512)
    conv_attn("wg", wg_d, wg_sc, 16)
    conv_attn("wo", wo_d, wo_sc, 16)

    def conv_ffn(name, dsrc, dscr, nk, nm):
        for ki in range(nk):
            s = ki % 2
            stch = f"pst_{name}{s}"
            em.wait("sync", "dve", em.mark("dve") - (1 if ki >= 1 else 0))
            em.dma("sync", pst_i[s][:, 0:nm * 128],
                   dsrc[128 * ki:128 * ki + 128, :], f"pld{s}")
            em.wait("vector", f"pld{s}")
            if st_gate[s]:
                em.wait("vector", st_gate[s][0], st_gate[s][1])
            TS("vector", pst_b[s][:, 0:nm * 128], pst_i[s][:, 0:nm * 128],
               -128.0, ALU.add, inc="dve")
            em.wait("scalar", "dve")
            src = pst_b[s][:, 0:nm * 128].rearrange("p (m c) -> p m c", m=nm)
            dst = dscr[:, ki, :, :].transpose([1, 0, 2])
            em.dma("scalar", dst, src, stch)
            st_gate[s] = (stch, em.mark(stch))
        pre_ready[name] = (f"pst_{name}0", em.mark(f"pst_{name}0"),
                           f"pst_{name}1", em.mark(f"pst_{name}1"))

    conv_ffn("gt", wqg_d, gt_sc, 16, 44)
    conv_ffn("up", wqu_d, up_sc, 16, 44)
    conv_ffn("dn", wqd_d, dn_sc, 44, 16)

    def wait_ready(eng, name):
        c0, v0, c1, v1 = pre_ready[name]
        em.wait(eng, c0, v0)
        em.wait(eng, c1, v1)

    def ln_stats_apply(cast_t, out_write):
        onecol = cbf_s[:, ONE:ONE + 1]
        onerow = cbf_s[0:1, ONE:ONE + 1].broadcast_to([1, 128])
        em.fence(("tensor",), ("dve", "pool", "act"))
        for p in range(16):
            MM(pb[0][0:1, :], onecol, pc(cast_t, p), p == 0, p == 15,
               inc="pe" if p == 15 else None)
        for p in range(16):
            sq = dmb3 if p % 2 == 0 else dmb2
            em.wait("scalar", "pe", max(0, em.mark("pe") - 1))
            ACT(sq[:], pc(cast_t, p), AF.Square, inc="act")
            em.wait("tensor", "act")
            MM(pb[1][0:1, :], onecol, sq[:], p == 0, p == 15, inc="pe")
        em.wait("scalar", "pe")
        ACT(rows_b[0:1, 0:512], pb[0][0:1, :], AF.Copy, scale=1.0 / D,
            inc="act")
        ACT(rows_f[0:1, 0:512], rows_b[0:1, 0:512], AF.Square, inc="act")
        em.wait("vector", "act")
        em.op("vector", lambda e: e.scalar_tensor_tensor(
            rows_f[0:1, 512:1024], pb[1][0:1, :], 1.0 / D,
            rows_f[0:1, 0:512], ALU.mult, ALU.subtract), inc="dve")
        em.wait("scalar", "dve")
        ACT(rows_f[0:1, 0:512], rows_f[0:1, 512:1024], AF.Ln,
            bias=cols_s[0:1, EPSC:EPSC + 1], inc="act")
        ACT(rows_b[0:1, 512:1024], rows_f[0:1, 0:512], AF.Exp, scale=-0.5,
            inc="act")
        em.wait("tensor", "act")
        MM(pb[2][:], onerow, rows_b[0:1, 0:512], True, True)
        MM(pb[3][:], onerow, rows_b[0:1, 512:1024], True, True, inc="pe")
        em.fence(("vector", "gpsimd"), ("pe",))
        for p in range(16):
            TT("vector", dmb1[:], pc(cast_t, p), pb[2][:], ALU.subtract,
               inc="dve")
            out_write(p)

    dbg = {}

    def dump(tag, ap, shape, ti, dt=BF16):
        if tag not in DEBUG.split(",") or ti != 0:
            return
        name = "dbg_" + tag
        t = nc.dram_tensor(name, list(shape), dt, kind="ExternalOutput").ap()
        dbg[name] = True
        em.fence(("gpsimd",), ("dve", "act", "pool", "pe"))
        em.dma("gpsimd", t, ap, "st")
        em.fence(("vector", "scalar", "gpsimd", "tensor", "sync"), ("st",))

    sidx = [0]

    def cut_out():
        em.fence(("gpsimd",), ("dve", "act", "pool", "pe"))
        em.dma("gpsimd", y_d[0, 0:128, 0:512], dm1[:], "st")


    def tile(ti):
        seq_start = (ti % 4) == 0
        b = ti // 4
        t0 = (ti % 4) * NT

        em.fence(("vector", "gpsimd", "scalar", "tensor", "sync"),
                 tuple(em.cnt.keys()))
        xle = xln[:].rearrange("p (c k) -> p c k", k=513)
        if seq_start:
            MEMSET("vector", xle[:, :, 0:1], 0.0, inc="dve")
        else:
            CP("vector", xle[:, :, 0:1], carry_s[:].unsqueeze(2), inc="dve")
        em.wait("gpsimd", "st")
        src = x_d[b, :, t0:t0 + NT].rearrange("(n p) c -> p n c", p=128)
        em.dma("gpsimd", xin[:].rearrange("p (n c) -> p n c", n=16), src,
               "xld1")
        em.fence(("scalar", "vector", "tensor"), ("xld1",))
        for p in range(16):
            ACT(pc(xb16, p), pc(xin, p), AF.Copy, inc="act")
        em.fence(("tensor", "vector"), ("act",))

        stt_mark = {}

        def ln1_write(p):
            TT("vector", xln[:, p * 513 + 1:p * 513 + 513], dmb1[:], pb[3][:],
               ALU.mult, inc="dve")
            TT("vector", pc(delta, p), xln[:, p * 513:p * 513 + 512],
               xln[:, p * 513 + 1:p * 513 + 513], ALU.subtract, inc="dve")
            STT("vector", pc(tmp_b, p), pc(delta, p), col(MUX + p),
                xln[:, p * 513 + 1:p * 513 + 513], ALU.mult, ALU.add,
                inc="dve")
            stt_mark[p] = em.mark("dve")
        ln_stats_apply(xb16, ln1_write)
        CP("vector", carry_s[:].unsqueeze(2), xle[:, :, 512:513], inc="dve")
        dump("xln", xln[:], [128, 16 * 513], ti)

        if CUT == "a1":
            return cut_out()
        em.fence(("tensor",), ("act",))
        for m in range(2):
            nm = 128 if m == 0 else 32
            for ki in range(16):
                if m == 0:
                    em.wait("tensor", "dve", stt_mark[ki])
                MM(pb[4 + m][0:nm, :],
                   wx1_s[:, ki * 160 + m * 128:ki * 160 + m * 128 + nm],
                   pc(tmp_b, ki), ki == 0, ki == 15,
                   inc="pe" if ki == 15 else None)
        em.wait("scalar", "pe")
        for m, (zt, nm) in enumerate(((zb0, 128), (zb1, 32))):
            em.wait("scalar", "dve")
            ACT(dm1[0:nm, :], pb[4 + m][0:nm, :], AF.Exp, scale=2.0, inc="act")
            em.wait("vector", "act")
            TS("vector", dm2[0:nm, :], dm1[0:nm, :], 1.0, ALU.add, inc="dve")
            em.op("vector", lambda e, nm=nm:
                  e.reciprocal(dm1[0:nm, :], dm2[0:nm, :]), inc="dve")
            TS("vector", zt[0:nm, :], dm1[0:nm, :], -2.0, ALU.mult, inc="dve",
               s2=1.0, op1=ALU.add)
        em.fence(("tensor",), ("dve",))

        mus_tt = {}
        mus_add = {}
        mix_done = {}

        def mus(i, dst_mix):
            em.fence(("tensor",), ("act", "dve"))
            if i < 4:
                lhs, base = wx2a_s, 32 * i
                zsl = zb0[32 * i:32 * i + 32, :]
            else:
                lhs, base = wx2b_s, 0
                zsl = zb1[0:32, :]
            for m in range(16):
                bank = m % 4
                if m - 4 in mus_tt:
                    em.wait("tensor", "dve", mus_tt[m - 4])
                MM(pb[bank][:], lhs[base:base + 32, m * 128:m * 128 + 128],
                   zsl, True, True, tp=(base, 0), inc="pe")
                em.wait("vector", "pe")
                if m in mus_add:
                    em.wait("vector", "dve", mus_add[m])
                TT("vector", pc(tmp_b, m), pb[bank][:], pc(delta, m), ALU.mult)
                mus_tt[m] = em.mark("dve")
                TT("vector", pc(dst_mix, m), pc(tmp_b, m),
                   xln[:, m * 513 + 1:m * 513 + 513], ALU.add)
                mus_add[m] = em.mark("dve")
            mix_done[i] = em.mark("dve")

        def proj(dscr, name, nm, mix, dst, mix_i=None, cap=99):
            nm = min(nm, cap)
            wait_ready("sync", name)
            if mix_i is not None:
                em.wait("tensor", "dve", mix_done[mix_i])
            evm = {}
            em.fence(("tensor",), ("act",))
            for m in range(nm):
                ch = f"wld{m % 2}"
                em.wait("sync", "pewc", max(0, em.mark("pewc") - 1))
                em.dma("sync", wcol[m % 2][:].rearrange("p (k c) -> p k c", k=16),
                       dscr[m].rearrange("k p c -> p k c"), ch)
                em.wait("tensor", ch)
                bank = 4 + (m % 2)
                if m - 2 in evm:
                    em.wait("tensor", "act", evm[m - 2])
                for ki in range(16):
                    MM(pb[bank][:], wcol[m % 2][:, ki * 128:ki * 128 + 128],
                       pc(mix, ki), ki == 0, ki == 15,
                       inc="pewc" if ki == 15 else None)
                em.wait("scalar", "pewc")
                if GDST and dst is sg_s:
                    ACT(dmb4[:], pb[bank][:], AF.Copy, inc="act")
                else:
                    ACT(pc(dst, m), pb[bank][:], AF.Copy, inc="act")
                evm[m] = em.mark("act")

        mus(0, xmix[0])
        proj(wr_sc, "wr", 8, xmix[0], r_raw, mix_i=0)
        mus(1, xmix[1])
        em.wait("tensor", "dve", mix_done[1])
        for ki in range(16):
            MM(pb[6][0:64, :], ww1_s[:, ki * 64:ki * 64 + 64], pc(xmix[1], ki),
               ki == 0, ki == 15, inc="pe" if ki == 15 else None)
        em.wait("scalar", "pe")
        em.wait("scalar", "dve")
        ACT(dm1[0:64, :], pb[6][0:64, :], AF.Exp, scale=2.0, inc="act")
        em.wait("vector", "act")
        TS("vector", dm2[0:64, :], dm1[0:64, :], 1.0, ALU.add, inc="dve")
        em.op("vector", lambda e: e.reciprocal(dm1[0:64, :], dm2[0:64, :]),
              inc="dve")
        TS("vector", tanha[0:64, :], dm1[0:64, :], -2.0, ALU.mult, inc="dve",
           s2=1.0, op1=ALU.add)
        MEMSET("vector", tanha[64:65, :], 1.0, inc="dve")
        em.fence(("tensor",), ("dve",))
        wem = {}
        em.fence(("tensor",), ("act",))
        for m in range(8):
            bank = 4 + (m % 2)
            if m - 2 in wem:
                em.wait("tensor", "act", wem[m - 2])
            MM(pb[bank][:], ww2_s[:, m * 128:m * 128 + 128], tanha[:],
               True, True, inc="pe")
            em.wait("scalar", "pe")
            ACT(pc(wexp, m), pb[bank][:], AF.Exp, inc="act")
            wem[m] = em.mark("act")
        mus(2, xmix[0])
        proj(wk_sc, "wk", 8, xmix[0], k_raw, mix_i=2)
        mus(3, xmix[1])
        wait_ready("sync", "wv")
        em.wait("tensor", "dve", mix_done[3])
        em.fence(("tensor",), ("dve", "act"))
        for vs in range(4):
            em.wait("tensor", "act")
            for ki in range(16):
                slot = ki % 2
                em.wait("sync", "pewc", max(0, em.mark("pewc") - 1))
                em.dma("sync", wcol[slot][:, 0:512], wv_sc[vs, ki],
                       f"wld{slot}")
                em.wait("tensor", f"wld{slot}")
                for tc in range(4):
                    MM(pb[tc][:],
                       xmix[1][:, ki * 512 + tc * 128:ki * 512 + tc * 128 + 128],
                       wcol[slot][:, 0:512], ki == 0, ki == 15,
                       inc="pewc" if tc == 3 else None)
            em.wait("scalar", "pewc")
            for tc in range(4):
                ACT(vtok[:, tc * 2048 + vs * 512:tc * 2048 + vs * 512 + 512],
                    pb[tc][:], AF.Copy, inc="act")
        mus(4, xmix[0])

        if CUT == "a2":
            return cut_out()
        zbc = col(ZRO).broadcast_to([128, 512])
        em.fence(("vector", "gpsimd", "scalar", "tensor"),
                 ("dve", "act", "pool", "pe"))

        for p in range(8):
            em.op("vector", lambda e, p=p: e.tensor_tensor_scan(
                pc(p512, p), zbc, pc(wexp, p), 0.0, ALU.add, ALU.add),
                inc="dve")
        if CUT == "a3scan":
            return cut_out()
        for p in range(8):
            CP("vector", psi_f[:, p * 18:p * 18 + 16],
               p512[:, p * 512 + 15:p * 512 + 512:32], inc="dve")
            CP("vector", psi_f[:, p * 18 + 16:p * 18 + 17],
               p512[:, p * 512 + 511:p * 512 + 512], inc="dve")
        em.fence(("scalar",), ("dve",))
        p3a = psi_f[:].rearrange("p (a c) -> p a c", a=8)
        TT("vector", dm1[:, 0:128].rearrange("p (a c) -> p a c", a=8),
           p3a[:, :, 0:16], p3a[:, :, 1:17], ALU.subtract, inc="dve")
        em.wait("scalar", "dve")
        for p in range(8):
            ACT(psi_b[:, p * 18:p * 18 + 16], dm1[:, p * 16:p * 16 + 16],
                AF.Exp, inc="act")
            ACT(psi_b[:, p * 18 + 17:p * 18 + 18],
                psi_f[:, p * 18:p * 18 + 1], AF.Exp, scale=-1.0, inc="act")
        if CUT == "a3psi":
            return cut_out()
        for p in range(8):
            mid = p512[:, p * 512 + 15:p * 512 + 512:32]
            midb = mid.unsqueeze(2).broadcast_to([128, 16, 32])
            p3 = pc(p512, p).rearrange("p (c j) -> p c j", c=16)
            em.wait("vector", "act" if p == 0 else "act")
            TT("vector", dm1[:].rearrange("p (c j) -> p c j", c=16), p3, midb,
               ALU.subtract, inc="dve")
            em.wait("scalar", "dve")
            ACT(pc(fk_s, p), dm1[:], AF.Exp, inc="act")
            em.wait("vector", "act")
            TT("vector", dm2[:], pc(wexp, p), dm1[:], ALU.subtract, inc="dve")
            em.wait("scalar", "dve")
            ACT(pc(fr_s, p), dm2[:], AF.Exp, inc="act")
        em.fence(("vector", "gpsimd"), ("act",))
        for p in range(8):
            TT("vector", pc(rt_s, p), pc(r_raw, p), pc(fr_s, p), ALU.mult,
               inc="dve")
            TT("vector", pc(kt_s, p), pc(k_raw, p), pc(fk_s, p), ALU.mult,
               inc="dve")
            TS("vector", pc(kb_s, p), pc(k_raw, p), col(BON + p), ALU.mult,
               inc="dve")
        em.fence(("vector",), ("dve",))
        for p in range(8):
            psv = psi_b[:, p * 18:p * 18 + 16].unsqueeze(2)
            psv = psv.broadcast_to([128, 16, 32])
            TT("vector", pc(khfm, p).rearrange("p (c j) -> p c j", c=16),
               pc(kt_s, p).rearrange("p (c j) -> p c j", c=16), psv, ALU.mult,
               inc="dve")
        if CUT == "a3fac":
            return cut_out()
        em.fence(("tensor",), ("dve",))
        tevm = {}
        for ii in range(32):
            kc, tc = ii // 4, ii % 4
            half = ii % 2
            if ii - 1 in tevm:
                em.wait("tensor", "dve", tevm[ii - 1])
            TRANS(tp_ps[:, half * 128:half * 128 + 128],
                  khfm[:, kc * 512 + tc * 128:kc * 512 + tc * 128 + 128],
                  inc="pe")
            em.wait("vector", "pe")
            CP("vector",
               kht[:, tc * 1024 + kc * 128:tc * 1024 + kc * 128 + 128],
               tp_ps[:, half * 128:half * 128 + 128], inc="dve")
            tevm[ii] = em.mark("dve")
        if CUT == "a3tp":
            return cut_out()
        si = sidx[0]
        if seq_start:
            MEMSET("vector", tstate[1 - si][:], 0.0, inc="dve")
        else:
            entv = psi_b[:, 17::18][:, 0:8]
            em.wait("vector", "act")
            TT("vector", tstate[1 - si][:].rearrange("p (a v) -> p a v", a=8),
               tstate[si][:].rearrange("p (a v) -> p a v", a=8),
               entv.unsqueeze(2).broadcast_to([128, 8, 64]), ALU.mult,
               inc="dve")
        sidx[0] = 1 - si
        if CUT == "a3ent":
            return cut_out()
        proj(wg_sc, "wg", 16, xmix[0], sg_s, mix_i=4, cap=GPN)

        dump("rraw", r_raw[:], [128, 8 * 512], ti)
        dump("kts", kt_s[:], [128, 8 * 512], ti)
        dump("rts", rt_s[:], [128, 8 * 512], ti)
        dump("kht", kht[:], [128, 4 * 1024], ti)
        dump("vtok", vtok[:], [128, 4 * 2048], ti)
        dump("sgs", sg_s[:], [128, 16 * 512], ti)
        dump("p512", p512[:], [128, 8 * 512], ti, dt=F32)
        dump("psib", psi_b[:], [128, 8 * 18], ti)
        if CUT == "a3":
            return cut_out()
        em.fence(("tensor", "vector", "gpsimd", "scalar"),
                 ("dve", "pool", "act", "pe"))
        msu = cbf_s[:, MSU:MSU + 32].unsqueeze(1).broadcast_to([128, 16, 32])
        msd = cbf_s[:, MSD:MSD + 32].unsqueeze(1).broadcast_to([128, 16, 32])
        cur = sidx[0]
        oev_dve = {}
        for blk in range(min(NBLK, max(SBLK, 1))):
            em.fence(("tensor",), ("dve", "pool", "act"))
            for sg in range(4):
                if sg > 0:
                    em.wait("tensor", "pe")
                strip = 32 * sg
                for h in range(sg, NH, 4):
                    pchk = h // 4
                    for ci in range(4):
                        c = blk * 4 + ci
                        co = pchk * 512 + c * 32
                        lk = kt_s[strip:strip + 32, co:co + 32]
                        lb = kb_s[strip:strip + 32, co:co + 32]
                        rr = rt_s[strip:strip + 32, co:co + 32]
                        rw = r_raw[strip:strip + 32, co:co + 32]
                        colo = 32 * (h % 16)
                        MM(pb[h // 16][32 * ci:32 * ci + 32, colo:colo + 32],
                           lk, rr, True, True, tp=(strip, 32 * ci))
                        MM(pb[2 + h // 16][32 * ci:32 * ci + 32,
                                           colo:colo + 32],
                           lb, rw, True, True, tp=(strip, 32 * ci),
                           inc="pe" if (h >= NH - 4 and ci == 3) else None)
            em.fence(("vector",), ("pe",))
            for half in range(2):
                TT("vector", dmb1[:].rearrange("p (a j) -> p a j", a=16),
                   pb[half][:].rearrange("p (a j) -> p a j", a=16), msu,
                   ALU.mult, inc="dve")
                TT("vector", dmb2[:].rearrange("p (a j) -> p a j", a=16),
                   pb[2 + half][:].rearrange("p (a j) -> p a j", a=16), msd,
                   ALU.mult, inc="dve")
                TT("vector", st_s[:, half * 512:half * 512 + 512], dmb1[:],
                   dmb2[:], ALU.add, inc="dve")
            if blk == 0:
                dump("sts0", st_s[:], [128, 1024], ti)
            if SPART == "sc":
                continue
            em.fence(("tensor",), ("dve",))
            for ci in range(4):
                c = blk * 4 + ci
                tc, tst = c // 4, 32 * (c % 4)
                sprev = tstate[cur]
                ibank = pb[ci]
                ebank = pb[4 + (ci % 2)]
                if ci >= 2:
                    em.wait("tensor", "dve", oev_dve.get(ci - 2, 0))
                for h in range(NH):
                    colo = (h // 2) * 32
                    lV = vtok[tst:tst + 32,
                              tc * 2048 + h * 64:tc * 2048 + h * 64 + 64]
                    rS = st_s[tst:tst + 32,
                              (h // 16) * 512 + 32 * (h % 16):
                              (h // 16) * 512 + 32 * (h % 16) + 32]
                    MM(ibank[64 * (h % 2):64 * (h % 2) + 64, colo:colo + 32],
                       lV, rS, True, True, tp=(tst, 64 * (h % 2)),
                       inc="pe" if h == NH - 1 else None)
                if SPART == "oi":
                    continue
                for sg in range(4):
                    em.wait("tensor", "pe")
                    strip = 32 * sg
                    for h in range(sg, NH, 4):
                        colo = (h // 2) * 32
                        lT = sprev[strip:strip + 32,
                                   (h // 4) * 64:(h // 4) * 64 + 64]
                        rr = rt_s[strip:strip + 32,
                                  (h // 4) * 512 + c * 32:
                                  (h // 4) * 512 + c * 32 + 32]
                        MM(ebank[64 * (h % 2):64 * (h % 2) + 64,
                                 colo:colo + 32],
                           lT, rr, True, True, tp=(strip, 64 * (h % 2)),
                           inc="pe" if h >= NH - 4 else None)
                em.wait("tensor", "pe")
                for h in range(NH):
                    strip = 32 * (h % 4)
                    lkh = kht[tst:tst + 32,
                              tc * 1024 + h * 32:tc * 1024 + h * 32 + 32]
                    rV = vtok[tst:tst + 32,
                              tc * 2048 + h * 64:tc * 2048 + h * 64 + 64]
                    MM(pb[6][strip:strip + 32,
                             (h // 4) * 64:(h // 4) * 64 + 64],
                       lkh, rV, True, True, tp=(tst, strip),
                       inc="pe" if h == NH - 1 else None)
                if SPART == "os":
                    continue
                em.fence(("vector", "scalar"), ("pe",))
                psv = psi_b[:, c::18][:, 0:8].unsqueeze(2)
                TT("vector", dmb3[:].rearrange("p (a v) -> p a v", a=8),
                   sprev[:].rearrange("p (a v) -> p a v", a=8),
                   psv.broadcast_to([128, 8, 64]), ALU.mult, inc="dve")
                TT("vector", tstate[1 - cur][:], dmb3[:], pb[6][:], ALU.add,
                   inc="dve")
                cur = 1 - cur
                em.fence(("tensor",), ("dve",))
                sq = dmb4 if ci % 2 == 0 else dmb5
                ACT(sq[:], ibank[:], AF.Copy, inc="act")
                em.wait("vector", "act")
                ob3c = ob16[:].rearrange("p (v c) -> p v c", v=16)
                TT("vector",
                   ob3c[:, :, blk * 128 + 32 * ci:blk * 128 + 32 * ci + 32],
                   sq[:].rearrange("p (v c) -> p v c", v=16), ebank[:]
                   .rearrange("p (v c) -> p v c", v=16), ALU.add, inc="dve")
                oev_dve[ci] = em.mark("dve")
        if SBLK < 99 or SPART:
            return cut_out()
        dump("ob16", ob16[:], [128, 16 * 512], ti)
        em.fence(("tensor",), ("act", "dve", "pool"))
        for vc in range(16):
            ep = cbf_s[:, EPA + 32 * vc:EPA + 32 * vc + 32]
            MM(pb[0][0:32, :], ep, pc(ob16, vc), vc == 0, vc == 15,
               inc="pe" if vc == 15 else None)
        em.wait("scalar", "pe")
        ACT(gnm_s[:], pb[0][0:32, :], AF.Copy, scale=1.0 / HV, inc="act")
        em.fence(("tensor",), ("act",))
        ocm = {}
        for vc in range(16):
            em.wait("tensor", "dve", max(0, em.mark("dve") - 3))
            MM(pb[2 + vc % 4][:], gpat_s[0:32, vc * 128:vc * 128 + 128],
               gnm_s[:], True, True, inc="pe")
            em.wait("vector", "pe")
            TT("vector", pc(oc_b, vc), pc(ob16, vc), pb[2 + vc % 4][:],
               ALU.subtract, inc="dve")
            ocm[vc] = em.mark("dve")
        for vc in range(16):
            sq = dmb3 if vc % 2 == 0 else dmb2
            em.wait("scalar", "dve", ocm[vc])
            em.wait("scalar", "pe", max(0, em.mark("pe") - 1))
            ACT(sq[:], pc(oc_b, vc), AF.Square, inc="act")
            em.wait("tensor", "act")
            MM(pb[1][0:32, :], cbf_s[:, EPA + 32 * vc:EPA + 32 * vc + 32],
               sq[:], vc == 0, vc == 15, inc="pe")
        em.wait("scalar", "pe")
        ACT(gnscr[:], pb[1][0:32, :], AF.Ln, bias=cols_s[0:32, EPSC:EPSC + 1],
            scale=1.0 / HV, inc="act")
        ACT(gnr_s[:], gnscr[:], AF.Exp, scale=-0.5, inc="act")
        em.fence(("tensor",), ("act",))
        for vc in range(16):
            sil = dmb4 if vc % 2 == 0 else dmb5
            mid = dmb1 if vc % 2 == 0 else dmb3
            em.wait("tensor", "dve", max(0, em.mark("dve") - 3))
            MM(pb[2 + vc % 4][:], gpat_s[0:32, vc * 128:vc * 128 + 128],
               gnr_s[:], True, True, inc="pe")
            em.wait("scalar", "dve", max(0, em.mark("dve") - 1))
            ACT(sil[:], pc(sg_s, vc), AF.Silu, inc="act")
            em.wait("vector", "pe")
            em.wait("vector", "act")
            STT("vector", mid[:], pc(oc_b, vc), col(GNW), pb[2 + vc % 4][:],
                ALU.mult, ALU.mult, inc="dve")
            STT("vector", pc(oprime, vc), mid[:], col(GNB), sil[:],
                ALU.add, ALU.mult, inc="dve")
        dump("oprime", oprime[:], [128, 16 * 512], ti)

        if CUT == "a4":
            return cut_out()
        em.fence(("vector", "gpsimd", "scalar", "tensor"),
                 ("dve", "act", "pool", "pe"))
        wait_ready("sync", "wo")
        em.fence(("tensor", "sync"), ("dve",))
        src = x_d[b, :, t0:t0 + NT].rearrange("(n p) c -> p n c", p=128)
        em.dma("gpsimd", xrel[:].rearrange("p (n c) -> p n c", n=16), src,
               "xld2")
        evm5 = {}
        em.fence(("tensor",), ("act", "dve"))
        for m in range(16):
            ch = f"wld{m % 2}"
            em.wait("sync", "pewc", max(0, em.mark("pewc") - 1))
            em.dma("sync", wcol[m % 2][:].rearrange("p (k c) -> p k c", k=16),
                   wo_sc[m].rearrange("k p c -> p k c"), ch)
            em.wait("tensor", ch)
            bank = 4 + (m % 2)
            if m - 2 in evm5:
                em.wait("tensor", "act", evm5[m - 2])
            for ki in range(16):
                MM(pb[bank][:], wcol[m % 2][:, ki * 128:ki * 128 + 128],
                   pc(oprime, ki), ki == 0, ki == 15,
                   inc="pewc" if ki == 15 else None)
            em.wait("scalar", "pewc")
            ACT(pc(ao_b, m), pb[bank][:], AF.Copy, inc="act")
            evm5[m] = em.mark("act")
        em.fence(("tensor", "vector", "scalar"), ("act",))
        em.fence(("vector", "gpsimd"), ("xld2",))

        def ln2_write(p):
            TT("vector", dmb2[:], dmb1[:], pb[3][:], ALU.mult, inc="dve")
            TT("vector", pc(xprime, p), pc(xrel, p), dmb2[:], ALU.add,
               inc="dve")
        ln_stats_apply(ao_b, ln2_write)
        dump("xprime", xprime[:], [128, 16 * 512], ti, dt=F32)

        if CUT == "a5":
            return cut_out()
        em.fence(("vector", "gpsimd", "scalar", "tensor"),
                 ("dve", "act", "pool", "pe"))

        for p in range(16):
            ACT(pc(xln3, p), pc(xprime, p), AF.Copy, inc="act")
        em.fence(("tensor", "vector"), ("act",))

        def ln3_write(p):
            TT("vector", pc(xln3, p), dmb1[:], pb[3][:], ALU.mult, inc="dve")
        ln_stats_apply(xln3, ln3_write)
        wait_ready("sync", "gt")
        wait_ready("sync", "up")
        em.fence(("tensor", "sync"), ("dve",))
        gum = {}
        gam = {}
        em.fence(("tensor",), ("act", "dve"))
        for m in range(44):
            ch = f"wld{m % 3}"
            if m - 3 in gum:
                em.wait("sync", "dve", gum[m - 3])
            em.dma("sync", gcol[m % 3][:].rearrange("p (k c) -> p k c", k=16),
                   gt_sc[m].rearrange("k p c -> p k c"), ch)
            em.dma("sync", ucol[m % 3][:].rearrange("p (k c) -> p k c", k=16),
                   up_sc[m].rearrange("k p c -> p k c"), ch)
            em.wait("tensor", ch)
            gb = pb[4] if m % 2 == 0 else pb[6]
            ub = pb[5] if m % 2 == 0 else pb[0]
            if m - 2 in gum:
                em.wait("tensor", "dve", gum[m - 2])
            if m - 2 in gam:
                em.wait("tensor", "act", gam[m - 2])
            for ki in range(16):
                MM(gb[:], gcol[m % 3][:, ki * 128:ki * 128 + 128],
                   pc(xln3, ki), ki == 0, ki == 15)
                MM(ub[:], ucol[m % 3][:, ki * 128:ki * 128 + 128],
                   pc(xln3, ki), ki == 0, ki == 15,
                   inc="pe" if ki == 15 else None)
            em.wait("scalar", "pe")
            sil = dmb1 if m % 2 == 0 else dmb2
            if m - 2 in gum:
                em.wait("scalar", "dve", gum[m - 2])
            ACT(sil[:], gb[:], AF.Silu, scale=col(SGA + m), inc="act")
            gam[m] = em.mark("act")
            em.wait("vector", "act")
            em.op("vector", lambda e, m=m, ub=ub, sil=sil: e.scalar_tensor_tensor(
                pc(gu_s, m), ub[:], col(SUP + m), sil[:], ALU.mult, ALU.mult))
            gum[m] = em.mark("dve")

        dump("xln3", xln3[:], [128, 16 * 512], ti)
        dump("gus", gu_s[:], [128, 44 * 512], ti)
        if CUT == "f1":
            return cut_out()
        em.fence(("vector", "gpsimd", "scalar", "tensor"),
                 ("dve", "act", "pool", "pe"))
        wait_ready("sync", "dn")
        em.fence(("tensor", "sync"), ("dve",))
        evm6 = {}
        em.fence(("tensor",), ("act",))
        for m in range(16):
            ch = f"wld{m % 2}"
            em.wait("sync", "pewc", max(0, em.mark("pewc") - 1))
            em.dma("sync", dcol[m % 2][:].rearrange("p (k c) -> p k c", k=44),
                   dn_sc[m].rearrange("k p c -> p k c"), ch)
            em.wait("tensor", ch)
            bank = 4 + (m % 2)
            if m - 2 in evm6:
                em.wait("tensor", "act", evm6[m - 2])
            for ki in range(44):
                MM(pb[bank][:], dcol[m % 2][:, ki * 128:ki * 128 + 128],
                   pc(gu_s, ki), ki == 0, ki == 43,
                   inc="pewc" if ki == 43 else None)
            em.wait("scalar", "pewc")
            ACT(pc(dnb, m), pb[bank][:], AF.Copy, scale=col(SDN + m),
                inc="act")
            evm6[m] = em.mark("act")
        em.fence(("tensor", "vector", "scalar"), ("act",))

        dump("dnb", dnb[:], [128, 16 * 512], ti)

        def ln4_write(p):
            TT("vector", dmb2[:], dmb1[:], pb[3][:], ALU.mult, inc="dve")
            TT("vector", pc(y_s, p), pc(xprime, p), dmb2[:], ALU.add,
               inc="dve")
        ln_stats_apply(dnb, ln4_write)
        em.fence(("gpsimd",), ("dve",))
        dst = y_d[b, :, t0:t0 + NT].rearrange("(n p) c -> p n c", p=128)
        em.dma("gpsimd", dst, y_s[:].rearrange("p (n c) -> p n c", n=16), "st")

    if CUT == "pre":
        em.fence(("gpsimd",), ("dve", "act", "pool", "pe", "pldc"))
        for chn, v0, ch1, v1 in [pre_ready[k] for k in pre_ready]:
            em.wait("gpsimd", chn, v0)
            em.wait("gpsimd", ch1, v1)
        em.dma("gpsimd", y_d[0, 0:128, 0:512], dm1[:], "st")
    else:
        for ti in range(n_tiles):
            tile(ti)

    em.wait("gpsimd", "st")
    em.wait("sync", "st")
    em.emit()
    return nc


def _host_inputs(inputs):
    x = np.ascontiguousarray(inputs["L0_attn_hidden_states"])
    mu_x = np.asarray(inputs["L0_attn_mu_x"], np.float32)
    Wx1 = np.asarray(inputs["L0_attn_Wx1"], np.float32)
    Wx2 = np.asarray(inputs["L0_attn_Wx2"], np.float32)
    Wr = np.ascontiguousarray(inputs["L0_attn_Wr"], np.float32)
    Ww1 = np.asarray(inputs["L0_attn_Ww1"], np.float32)
    Ww2 = np.asarray(inputs["L0_attn_Ww2"], np.float32)
    bw = np.asarray(inputs["L0_attn_bw"], np.float32)
    Wk = np.ascontiguousarray(inputs["L0_attn_Wk"], np.float32)
    Wv = np.ascontiguousarray(inputs["L0_attn_Wv"], np.float32)
    Wg = np.ascontiguousarray(inputs["L0_attn_Wg"], np.float32)
    bonus = np.asarray(inputs["L0_attn_bonus"], np.float32)
    gn_w = np.asarray(inputs["L0_attn_gn_w"], np.float32)
    gn_b = np.asarray(inputs["L0_attn_gn_b"], np.float32)
    Wo = np.ascontiguousarray(inputs["L0_attn_Wo"], np.float32)
    wqg = np.asarray(inputs["L0_ffn_wq_gate"], np.int32)
    sgate = np.asarray(inputs["L0_ffn_s_gate"], np.float32)
    wqu = np.asarray(inputs["L0_ffn_wq_up"], np.int32)
    sup = np.asarray(inputs["L0_ffn_s_up"], np.float32)
    wqd = np.asarray(inputs["L0_ffn_wq_down"], np.int32)
    sdown = np.asarray(inputs["L0_ffn_s_down"], np.float32)

    wx1_l = np.ascontiguousarray(
        Wx1.reshape(16, 128, 160).transpose(1, 0, 2).reshape(128, 16 * 160))
    wx2a_l = np.ascontiguousarray(Wx2[0:4].reshape(128, D))
    wx2b_l = np.ascontiguousarray(Wx2[4])
    ww1_l = np.ascontiguousarray(
        Ww1.reshape(16, 128, 64).transpose(1, 0, 2).reshape(128, 16 * 64))
    ww2_l = np.ascontiguousarray(np.concatenate([Ww2, bw[None, :]], axis=0))
    wqg_t = np.ascontiguousarray(wqg.T)
    wqu_t = np.ascontiguousarray(wqu.T)
    wqd_t = np.ascontiguousarray(wqd.T)

    cols = np.zeros((128, NCOLS), np.float32)
    cols[:, BON:BON + 8] = bonus.reshape(-1).reshape(8, 128).T
    cols[:, MUX:MUX + 16] = mu_x.reshape(16, 128).T
    cols[:, GNW] = np.tile(gn_w, 2)
    cols[:, GNB] = np.tile(gn_b, 2)
    cols[:, SGA:SGA + 44] = sgate.reshape(44, 128).T
    cols[:, SUP:SUP + 44] = sup.reshape(44, 128).T
    cols[:, SDN:SDN + 16] = sdown.reshape(16, 128).T
    cols[:, EPSC] = EPS

    cbf = np.zeros((128, NBF), np.float32)
    j = np.arange(32)
    m32 = (j[:, None] < j[None, :]).astype(np.float32)
    d32 = np.eye(32, dtype=np.float32)
    cbf[:, MSU:MSU + 32] = np.tile(m32, (4, 1))
    cbf[:, MSD:MSD + 32] = np.tile(d32, (4, 1))
    cbf[:, IDE:IDE + 128] = np.eye(128, dtype=np.float32)
    for vc in range(16):
        ep = np.zeros((128, 32), np.float32)
        ep[0:64, 2 * vc] = 1.0
        ep[64:128, 2 * vc + 1] = 1.0
        cbf[:, EPA + 32 * vc:EPA + 32 * vc + 32] = ep
    cbf[:, ONE] = 1.0
    cbf[:, ONE + 1] = 1.0
    cbf_b = cbf.astype(bf16)

    gpat = np.zeros((32, 2048), np.float32)
    for pcn in range(16):
        gp = np.zeros((32, 128), np.float32)
        gp[2 * pcn, 0:64] = 1.0
        gp[2 * pcn + 1, 64:128] = 1.0
        gpat[:, 128 * pcn:128 * pcn + 128] = gp
    gpat_b = gpat.astype(bf16)

    shared = dict(wx1=wx1_l, wx2a=wx2a_l, wx2b=wx2b_l, ww1=ww1_l,
                  ww2aug=ww2_l, wr=Wr, wk=Wk, wv=Wv, wg=Wg, wo=Wo,
                  wqg=wqg_t, wqu=wqu_t, wqd=wqd_t, cols=cols, cbf=cbf_b,
                  gpat=gpat_b)
    maps = []
    for c in range(8):
        xt = np.ascontiguousarray(x[2 * c:2 * c + 2].transpose(0, 2, 1))
        m = dict(shared)
        m["x"] = xt
        maps.append(m)
    return maps


def kernel(**inputs):
    nc = build(N_TILES)
    in_maps = _host_inputs(inputs)
    res = run_bass_kernel_spmd(nc, in_maps, core_ids=list(range(8)))
    y = np.empty((16, LSEQ, D), np.float32)
    for c in range(8):
        yc = np.asarray(res.results[c]["y"])
        y[2 * c:2 * c + 2] = yc.transpose(0, 2, 1)
    return y
